# version 44
# baseline (speedup 1.0000x reference)
import numpy as np
import ml_dtypes
from contextlib import ExitStack

import concourse.bass as bass
import concourse.mybir as mybir
from concourse.bass_utils import run_bass_kernel_spmd

F32 = mybir.dt.float32
BF16 = mybir.dt.bfloat16
AF = mybir.ActivationFunctionType
ALU = mybir.AluOpType
AX = mybir.AxisListType

B, T, C = 2, 8192, 1024
NH, HS, KVL, QL = 16, 64, 256, 256
NSLOT, QB = 4, 512
NKG = T // 512
NKC = T // 128
EPS = 1e-5
BF = ml_dtypes.bfloat16
I32 = mybir.dt.int32


def build_nc():
    nc = bass.Bass("TRN2", target_bir_lowering=False)

    xk = nc.dram_tensor("xk", [T, C], F32, kind="ExternalInput").ap()
    xq = nc.dram_tensor("xq", [NSLOT * QB, C], F32, kind="ExternalInput").ap()
    masks = nc.dram_tensor("masks", [NSLOT, 128, 16 * 512], BF16, kind="ExternalInput").ap()
    w_dq = nc.dram_tensor("w_dq", [QL, C], F32, kind="ExternalInput").ap()
    w_uq = nc.dram_tensor("w_uq", [C, QL], F32, kind="ExternalInput").ap()
    w_dkv = nc.dram_tensor("w_dkv", [KVL, C], F32, kind="ExternalInput").ap()
    w_uk = nc.dram_tensor("w_uk", [C, KVL], F32, kind="ExternalInput").ap()
    w_uv = nc.dram_tensor("w_uv", [C, KVL], F32, kind="ExternalInput").ap()
    w_o = nc.dram_tensor("w_o", [C, C], F32, kind="ExternalInput").ap()
    fk = nc.dram_tensor("fk", [C, C], F32, kind="ExternalInput").ap()
    fbias = nc.dram_tensor("fbias", [1, C], F32, kind="ExternalInput").ap()
    fa = nc.dram_tensor("fa", [C, 16], F32, kind="ExternalInput").ap()
    fb = nc.dram_tensor("fb", [16, C], F32, kind="ExternalInput").ap()
    ident_b = nc.dram_tensor("ident_b", [128, 128], BF16, kind="ExternalInput").ap()
    ident_f = nc.dram_tensor("ident_f", [128, 128], F32, kind="ExternalInput").ap()
    out_hbm = nc.dram_tensor("out", [NSLOT * QB, C], F32, kind="ExternalOutput").ap()

    with ExitStack() as stack:
        E = stack.enter_context

        ckvT = [E(nc.sbuf_tensor(f"ckvT{i}", [128, T], BF16)) for i in range(2)]
        ckvkm = E(nc.sbuf_tensor("ckvkm", [128, NKC * 256], BF16))
        masks_sb = E(nc.sbuf_tensor("masks_sb", [128, 16 * 512], BF16))
        xq_f = [E(nc.sbuf_tensor(f"xq_f{i}", [128, C], F32)) for i in range(2)]
        xqT = E(nc.sbuf_tensor("xqT", [128, 8 * QB], BF16))
        qT = E(nc.sbuf_tensor("qT", [128, 8 * QB], BF16))
        qlat = [[E(nc.sbuf_tensor(f"qlat{p}{i}", [128, QB], BF16)) for i in range(2)]
                for p in range(2)]
        PT = [E(nc.sbuf_tensor(f"PT{i}", [128, QB], BF16)) for i in range(4)]
        Pacc = [E(nc.sbuf_tensor(f"Pacc{i}", [128, QB], BF16)) for i in range(2)]
        ctxn = [[E(nc.sbuf_tensor(f"ctxn{i}{l}", [128, QB], BF16)) for l in range(2)]
                for i in range(2)]
        y_sb = E(nc.sbuf_tensor("y_sb", [128, 8 * QB], BF16))
        x1b = [E(nc.sbuf_tensor(f"x1b{i}", [128, C], BF16)) for i in range(2)]
        x1T = E(nc.sbuf_tensor("x1T", [128, C], BF16))
        r2 = [E(nc.sbuf_tensor(f"r2_{i}", [128, C], F32)) for i in range(2)]
        out_st = [E(nc.sbuf_tensor(f"out_st{i}", [128, C], F32)) for i in range(2)]
        wdqT = E(nc.sbuf_tensor("wdqT", [128, 8 * QL], BF16))
        wuqT = [E(nc.sbuf_tensor(f"wuqT{i}", [128, C], BF16)) for i in range(2)]
        wdkvT = E(nc.sbuf_tensor("wdkvT", [128, 8 * KVL], BF16))
        keff = E(nc.sbuf_tensor("keff", [128, 8 * KVL], BF16))
        vsb = [E(nc.sbuf_tensor(f"vsb{i}", [128, C], BF16)) for i in range(2)]
        wq = [E(nc.sbuf_tensor(f"wq{i}", [128, C], BF16)) for i in range(8)]
        a_sb = E(nc.sbuf_tensor("a_sb", [128, 8 * 16], BF16))
        b_sb = E(nc.sbuf_tensor("b_sb", [16, C], BF16))
        bias_sb = E(nc.sbuf_tensor("bias_sb", [1, C], BF16))
        id_b = E(nc.sbuf_tensor("id_b", [128, 128], BF16))
        id_f = E(nc.sbuf_tensor("id_f", [128, 128], F32))
        ones_col = E(nc.sbuf_tensor("ones_col", [128, 1], BF16))
        ones_row_b = E(nc.sbuf_tensor("ones_row_b", [1, 128], BF16))
        ones_row_f = E(nc.sbuf_tensor("ones_row_f", [1, 128], F32))
        uT = E(nc.sbuf_tensor("uT", [16, 128], BF16))
        stats = E(nc.sbuf_tensor("stats", [128, 24], F32))
        smax = E(nc.sbuf_tensor("smax", [128, 16], F32))
        negc = E(nc.sbuf_tensor("negc", [128, 16], F32))
        kfac = E(nc.sbuf_tensor("kfac", [128, 1], F32))
        eps_t = E(nc.sbuf_tensor("eps_t", [128, 1], F32))
        m7 = E(nc.sbuf_tensor("m7", [128, 1], F32))
        r7 = E(nc.sbuf_tensor("r7", [128, 1], F32))
        sc1 = E(nc.sbuf_tensor("sc1", [1, 12], F32))
        recip = [E(nc.sbuf_tensor(f"recip{p}", [1, QB], F32)) for p in range(2)]
        recip_b = [E(nc.sbuf_tensor(f"recip_b{p}", [1, QB], BF16)) for p in range(2)]
        rbc = [E(nc.sbuf_tensor(f"rbc{p}", [128, QB], F32)) for p in range(2)]
        den_sb = [E(nc.sbuf_tensor(f"den{p}", [1, QB], F32)) for p in range(2)]
        stairT = E(nc.sbuf_tensor("stairT", [128, C], BF16))

        def xqTc(dc):
            return xqT[:, QB * dc:QB * (dc + 1)]

        def qTc(dc):
            return qT[:, QB * dc:QB * (dc + 1)]

        PS_S3 = [E(nc.psum_tensor(f"ps_s{i}", [128, QB], F32)) for i in range(3)]
        PS_C = [E(nc.psum_tensor(f"ps_c{i}", [128, QB], F32)) for i in range(2)]
        PS_SUM = E(nc.psum_tensor("ps_sum", [128, QB], F32))
        PS_M = E(nc.psum_tensor("ps_m", [128, QB], F32))
        PB0 = E(nc.psum_tensor("pb0", [128, 1024], BF16))
        PS_S = [t[:] for t in PS_S3] + [PB0.ap().bitcast(F32)]

        class Sem:
            def __init__(self, name):
                self.h = E(nc.semaphore(name))
                self.n = 0

        LD = Sem("LD")
        SA = Sem("SA")
        SB = Sem("SB")
        LQA = Sem("LQA")
        LQB = Sem("LQB")
        TGA = Sem("TGA")
        TGB = Sem("TGB")
        TR = Sem("TR")
        PE = Sem("PE")
        AC = Sem("AC")
        DV = Sem("DV")
        OT = Sem("OT")

        plan = {e: [] for e in ("tensor", "scalar", "vector", "sync", "gpsimd")}

        def P(e, fn):
            plan[e].append(fn)

        def W(e, sem, val):
            if val > 0:
                P(e, lambda eng, sh=sem.h, v=val: eng.wait_ge(sh, v))

        def barrier():
            vals = [(sm, sm.n) for sm in (LD, SA, SB, LQA, LQB, TGA, TGB, TR, PE, AC, DV, OT)]
            for e in plan:
                for sm, v in vals:
                    W(e, sm, v)

        def I(e, fn, sem=None, k=1):
            if sem is None:
                P(e, lambda eng, f=fn: f(eng))
                return None
            sem.n += k
            P(e, lambda eng, f=fn, sh=sem.h, kk=k: f(eng).then_inc(sh, kk))
            return sem.n

        def VE(fn):
            v = I("vector", fn, DV)
            W("vector", DV, v)
            return v

        def SE(fn):
            v = I("scalar", fn, AC)
            W("scalar", AC, v)
            return v

        def transpose_tile_to_xT(src_bf, dst_view3):
            for dc in range(8):
                I("tensor", lambda e, dc=dc: e.transpose(
                    PB0[:, 128 * dc:128 * (dc + 1)], src_bf[:, 128 * dc:128 * (dc + 1)], id_b[:]),
                    PE if dc == 7 else None)
            pe_v = PE.n
            W("vector", PE, pe_v)
            dv_v = I("vector", lambda e, d=dst_view3: e.tensor_copy(
                d, PB0.ap().rearrange("p (d q) -> p d q", d=8)), DV)
            return pe_v, dv_v

        wdq_nat = [masks_sb[:, i * C:(i + 1) * C] for i in range(2)]
        wuq_nat8 = [masks_sb[:, 2 * C + i * 256: 2 * C + (i + 1) * 256] for i in range(8)]
        wuk_nat = [masks_sb[:, 4 * C + i * 256: 4 * C + (i + 1) * 256] for i in range(8)]
        wuv_nat = [ckvkm[:, 8 * C + i * 256: 8 * C + (i + 1) * 256] for i in range(8)]
        wdkv_natb = [y_sb[:, 0:C], y_sb[:, C:2 * C]]
        ke1 = [y_sb[:, 2 * C + i * 256: 2 * C + (i + 1) * 256] for i in range(2)]
        stair_ph0 = [ckvkm[:, 12 * C:13 * C], ckvkm[:, 13 * C:14 * C]]
        qint = ckvkm[:, 14 * C:16 * C].bitcast(I32)
        x1f_s = ckvkm[:, 4 * C:6 * C].bitcast(F32)

        I("gpsimd", lambda e: e.dma_start(out=id_b[:], in_=ident_b), LD, 16)
        I("gpsimd", lambda e: e.dma_start(out=id_f[:], in_=ident_f), LD, 16)
        I("vector", lambda e: e.memset(ones_col[:], 1.0))
        I("vector", lambda e: e.memset(ones_row_b[:], 1.0))
        I("vector", lambda e: e.memset(ones_row_f[:], 1.0))
        I("vector", lambda e: e.memset(sc1[0:1, 3:4], 0.0))
        I("vector", lambda e: e.memset(eps_t[:], EPS), DV)

        def batch8_transpose(srcs, dst3):
            W("tensor", DV, DV.n)
            for dc in range(8):
                I("tensor", lambda e, s2=srcs[dc], dc=dc: e.transpose(
                    PB0[:, 128 * dc:128 * (dc + 1)], s2, id_b[:]), PE if dc == 7 else None)
            W("vector", PE, PE.n)
            v = I("vector", lambda e, d=dst3: e.tensor_copy(
                d, PB0.ap().rearrange("p (d q) -> p d q", d=8)), DV)
            W("tensor", DV, v)
            return v

        stage_jobs = []
        stage_jobs += [("wdq%d" % i, w_dq[128 * i:128 * (i + 1), :], C, wdq_nat[i]) for i in range(2)]
        stage_jobs += [("wdkv%d" % i, w_dkv[128 * i:128 * (i + 1), :], C, wdkv_natb[i]) for i in range(2)]
        stage_jobs += [("wuq%d" % i, w_uq[128 * i:128 * (i + 1), :], QL, wuq_nat8[i]) for i in range(8)]
        stage_jobs += [("wuk%d" % i, w_uk[128 * i:128 * (i + 1), :], KVL, wuk_nat[i]) for i in range(8)]
        stage_jobs += [("wuv%d" % i, w_uv[128 * i:128 * (i + 1), :], KVL, wuv_nat[i]) for i in range(8)]
        stage_jobs += [("fa%d" % i, fa[128 * i:128 * (i + 1), :], 16, a_sb[:, 16 * i:16 * (i + 1)]) for i in range(8)]
        wo_b = [masks_sb[:, i * C:(i + 1) * C] for i in range(8)]
        wo_jobs = [("wo%d" % i, w_o[128 * i:128 * (i + 1), :], C, wo_b[i]) for i in range(8)]

        cast_of = {}
        stage_ctr = {}
        stage_prev = []

        def plan_stage(jobs, k=1, gate=None):
            key = id(jobs)
            stage_ctr.setdefault(key, 0)
            for _ in range(k):
                if stage_ctr[key] >= len(jobs):
                    return
                name, src2, ncol, dst = jobs[stage_ctr[key]]
                stage_ctr[key] += 1
                gi = len(stage_prev)
                buf = r2[gi % 2]
                if gi >= 2:
                    W("gpsimd", AC, cast_of[stage_prev[gi - 2]])
                if gate is not None:
                    W("gpsimd", PE, gate)
                sm = (SA, SB)[gi % 2]
                ld_v = I("gpsimd", lambda e, b=buf, s2=src2, n=ncol: e.dma_start(out=b[:, 0:n], in_=s2), sm, 16)
                W("scalar", sm, ld_v)
                cast_of[name] = I("scalar", lambda e, b=buf, d=dst, n=ncol: e.activation(
                    out=d, in_=b[:, 0:n], func=AF.Copy), AC)
                stage_prev.append(name)

        x_cast, x_trb, x_mm = {}, {}, {}

        def x_piece(t):
            if t >= 2:
                W("sync", AC, x_cast[t - 2])
            ld_v = I("sync", lambda e, t=t: e.dma_start(
                out=xq_f2[t % 2][:], in_=xk[128 * t:128 * (t + 1), :]), TR, 16)
            W("scalar", TR, ld_v)
            if t >= 2:
                W("scalar", PE, x_trb[t - 2])
            x_cast[t] = I("scalar", lambda e, t=t: e.activation(
                out=x1b[t % 2][:], in_=xq_f2[t % 2][:], func=AF.Copy), AC)
            W("tensor", AC, x_cast[t])
            W("tensor", DV, DV.n)
            for dc in range(8):
                I("tensor", lambda e, dc=dc, t=t: e.transpose(
                    PB0[:, 128 * dc:128 * (dc + 1)], x1b[t % 2][:, 128 * dc:128 * (dc + 1)],
                    id_b[:]), PE if dc == 7 else None)
            x_trb[t] = PE.n
            W("vector", PE, PE.n)
            if t >= 2:
                W("vector", PE, x_mm[t - 2])
            dv = I("vector", lambda e, t=t: e.tensor_copy(stair_ph0[t % 2], PB0[:]), DV)
            W("tensor", DV, dv)
            for latc in range(2):
                for dc in range(8):
                    I("tensor", lambda e, latc=latc, dc=dc, t=t: e.matmul(
                        PS_M[:, 128 * latc:128 * (latc + 1)],
                        wdkvT[:, 256 * dc + 128 * latc:256 * dc + 128 * (latc + 1)],
                        stair_ph0[t % 2][:, 128 * dc:128 * (dc + 1)],
                        start=(dc == 0), stop=(dc == 7), skip_group_check=True))
            for dc in range(8):
                I("tensor", lambda e, dc=dc, t=t: e.matmul(
                    PS_M[:, 256:512], stair_ph0[t % 2][:, 128 * dc:128 * (dc + 1)],
                    wdkvT[:, 256 * dc:256 * (dc + 1)],
                    start=(dc == 0), stop=(dc == 7), skip_group_check=True),
                    PE if dc == 7 else None)
            x_mm[t] = PE.n
            W("scalar", PE, PE.n)
            ac2 = None
            for latc in range(2):
                ac2 = I("scalar", lambda e, latc=latc, t=t: e.activation(
                    out=ckvT[latc][:, 128 * t:128 * (t + 1)],
                    in_=PS_M[:, 128 * latc:128 * (latc + 1)], func=AF.Copy), AC)
            W("vector", AC, ac2)
            dv = I("vector", lambda e, t=t: e.tensor_copy(
                ckvkm[:, 256 * t:256 * (t + 1)], PS_M[:, 256:512]), DV)
            W("tensor", AC, ac2)
            W("tensor", DV, dv)

        xq_f2 = xq_f

        qbufs = [out_st[0], out_st[1]]
        red_hist = []

        def plan_quant_load_reduce(i):
            buf = qbufs[i % 2]
            sm = (LQA, LQB)[i % 2]
            if i >= 2:
                W("sync", DV, red_hist[i - 2])
            ld_v = I("sync", lambda e, b=buf, i=i: e.dma_start(out=b[:, 0:C], in_=fk[128 * i:128 * (i + 1), :]), sm, 16)
            W("vector", sm, ld_v)
            v = I("vector", lambda e, b=buf, i=i: e.tensor_reduce(
                smax[:, i:i + 1], b[:, 0:C], AX.X, ALU.max, apply_absolute_value=True), DV)
            red_hist.append(v)

        def plan_quant_scale():
            W("vector", DV, red_hist[-1])
            VE(lambda e: e.tensor_reduce(smax[:, 8:9], smax[:, 0:8], AX.X, ALU.max))
            W("tensor", DV, DV.n)
            v = I("tensor", lambda e: e.transpose(PS_M[0:1, 0:128], smax[:, 8:9], id_f[:]), PE)
            W("vector", PE, v)
            VE(lambda e: e.tensor_reduce(sc1[0:1, 0:1], PS_M[0:1, 0:128], AX.X, ALU.max))
            VE(lambda e: e.tensor_scalar(out=sc1[0:1, 1:2], in0=sc1[0:1, 0:1],
                                         scalar1=1.0 / 7.0, scalar2=None, op0=ALU.mult))
            v = VE(lambda e: e.reciprocal(sc1[0:1, 2:3], sc1[0:1, 1:2]))
            W("tensor", DV, v)
            I("tensor", lambda e: e.matmul(PS_M[:, 0:1], ones_row_f[:], sc1[0:1, 1:2],
                                           start=True, stop=True, skip_group_check=True), PE)
            v = I("tensor", lambda e: e.matmul(PS_M[:, 1:2], ones_row_f[:], sc1[0:1, 2:3],
                                               start=True, stop=True, skip_group_check=True), PE)
            W("scalar", PE, v)
            I("scalar", lambda e: e.activation(out=m7[:], in_=PS_M[:, 0:1], func=AF.Copy), AC)
            v = I("scalar", lambda e: e.activation(out=r7[:], in_=PS_M[:, 1:2], func=AF.Copy), AC)
            W("vector", AC, v)
            W("tensor", AC, v)

        lad_hist = []

        def plan_quant_round(i):
            buf = qbufs[i % 2]
            sm = (LQA, LQB)[i % 2]
            if i >= 2:
                W("sync", DV, lad_hist[i - 2])
            ld_v = I("sync", lambda e, b=buf, i=i: e.dma_start(out=b[:, 0:C], in_=fk[128 * i:128 * (i + 1), :]), sm, 16)
            W("vector", sm, ld_v)
            VE(lambda e, b=buf: e.tensor_scalar(out=x1f_s[:, 0:C], in0=b[:, 0:C],
                                                scalar1=r7[:], scalar2=None, op0=ALU.mult))
            VE(lambda e: e.tensor_copy(qint[:, 0:C], x1f_s[:, 0:C]))
            v = VE(lambda e, i=i: e.tensor_scalar(
                out=wq[i][:], in0=qint[:, 0:C], scalar1=m7[:], scalar2=None, op0=ALU.mult))
            lad_hist.append(v)

        plan_stage(stage_jobs, 4)
        W("tensor", AC, cast_of["wdkv1"])
        for latc in range(2):
            batch8_transpose([wdkv_natb[latc][:, 128 * dc:128 * (dc + 1)] for dc in range(8)],
                             wdkvT.ap().rearrange("p (d t q) -> p d t q", t=2, q=128)[:, :, latc, :])
        for t in range(16):
            x_piece(t)
            plan_stage(stage_jobs, 2)
            if t == 2:
                W("tensor", AC, cast_of["wdq1"])
                for qlc in range(2):
                    batch8_transpose([wdq_nat[qlc][:, 128 * dc:128 * (dc + 1)] for dc in range(8)],
                                     wdqT.ap().rearrange("p (d t q) -> p d t q", t=2, q=128)[:, :, qlc, :])
            elif t == 5:
                W("tensor", AC, cast_of["wuq7"])
                for qlc in range(2):
                    batch8_transpose([wuq_nat8[dc][:, 128 * qlc:128 * (qlc + 1)] for dc in range(8)],
                                     wuqT[qlc].ap().rearrange("p (d q) -> p d q", d=8))
            elif t == 7:
                W("tensor", AC, cast_of["wuk7"])
                for qlc in range(2):
                    for dc in range(8):
                        I("tensor", lambda e, qlc=qlc, dc=dc: e.matmul(
                            PS_M[:, 0:256], wuq_nat8[dc][:, 128 * qlc:128 * (qlc + 1)], wuk_nat[dc],
                            start=(dc == 0), stop=(dc == 7), skip_group_check=True),
                            PE if dc == 7 else None)
                    W("scalar", PE, PE.n)
                    v = I("scalar", lambda e, qlc=qlc: e.activation(out=ke1[qlc], in_=PS_M[:, 0:256], func=AF.Copy), AC)
                    W("tensor", AC, v)
            elif t == 8:
                keff_pe = 0
                for cc in range(8):
                    for qlc in range(2):
                        I("tensor", lambda e, cc=cc, qlc=qlc: e.matmul(
                            PS_M[:, 0:256], wdq_nat[qlc][:, 128 * cc:128 * (cc + 1)], ke1[qlc],
                            start=(qlc == 0), stop=(qlc == 1), skip_group_check=True),
                            PE if qlc == 1 else None)
                    keff_pe = PE.n
                    W("scalar", PE, PE.n)
                    v = I("scalar", lambda e, cc=cc: e.activation(
                        out=keff[:, 256 * cc:256 * (cc + 1)], in_=PS_M[:, 0:256], func=AF.Copy), AC)
                    W("tensor", AC, v)
            elif t == 9:
                plan_stage(wo_jobs, 2, gate=keff_pe)
                plan_quant_load_reduce(0)
                plan_quant_load_reduce(1)
            elif t == 10:
                plan_stage(wo_jobs, 3)
                for i in range(2, 5):
                    plan_quant_load_reduce(i)
            elif t == 11:
                plan_stage(wo_jobs, 3)
                for i in range(5, 8):
                    plan_quant_load_reduce(i)
            elif t == 12:
                plan_quant_scale()
            elif t >= 13:
                plan_quant_round(2 * (t - 13))
                plan_quant_round(2 * (t - 13) + 1)
        plan_stage(stage_jobs, 99)
        for i in range(6, 8):
            plan_quant_round(i)
        W("gpsimd", AC, cast_of[stage_prev[-2]])
        ld_v = I("gpsimd", lambda e: e.dma_start(out=r2[0][0:16, 0:C], in_=fb), SA, 16)
        W("scalar", SA, ld_v)
        I("scalar", lambda e: e.activation(out=b_sb[:], in_=r2[0][0:16, 0:C], func=AF.Copy), AC)
        W("gpsimd", AC, cast_of[stage_prev[-1]])
        ld_v = I("gpsimd", lambda e: e.dma_start(out=r2[1][0:1, 0:C], in_=fbias), SB, 16)
        W("scalar", SB, ld_v)
        I("scalar", lambda e: e.activation(out=bias_sb[:], in_=r2[1][0:1, 0:C], func=AF.Copy), AC)
        W("tensor", AC, cast_of["wo7"])
        woT = []
        for dc in range(8):
            if dc < 4:
                woT.append([qTc(2 * dc), qTc(2 * dc + 1)])
            else:
                woT.append([xqTc(2 * (dc - 4)), xqTc(2 * (dc - 4) + 1)])
        for cc in range(8):
            half, off = divmod(128 * cc, QB)
            W("tensor", DV, DV.n)
            W("tensor", AC, AC.n)
            for dc in range(8):
                I("tensor", lambda e, cc=cc, dc=dc: e.transpose(
                    PB0[:, 128 * dc:128 * (dc + 1)], wo_b[cc][:, 128 * dc:128 * (dc + 1)],
                    id_b[:]), PE if dc == 7 else None)
            W("scalar", PE, PE.n)
            v = None
            for dd in range(8):
                dst = (qT if dd < 4 else xqT)[:, 1024 * (dd % 4) + 512 * half + off:
                                              1024 * (dd % 4) + 512 * half + off + 128]
                v = I("scalar", lambda e, dd=dd, dst=dst: e.activation(
                    out=dst, in_=PB0[:, 128 * dd:128 * (dd + 1)], func=AF.Copy), AC)
            W("tensor", AC, v)
        W("tensor", AC, cast_of["wuv7"])
        for lc in range(2):
            for nb in range(2):
                for dc in range(8):
                    I("tensor", lambda e, lc=lc, nb=nb, dc=dc: e.matmul(
                        PS_M[:], wuv_nat[dc][:, 128 * lc:128 * (lc + 1)], woT[dc][nb],
                        start=(dc == 0), stop=(dc == 7), skip_group_check=True),
                        PE if dc == 7 else None)
                W("scalar", PE, PE.n)
                v = I("scalar", lambda e, lc=lc, nb=nb: e.activation(
                    out=vsb[lc][:, QB * nb:QB * (nb + 1)], in_=PS_M[:], func=AF.Copy), AC)
                W("tensor", AC, v)

        barrier()

        n_out_dma = 0
        last_stair_pe = [0]
        qprep_pe_hold = [0]
        den_last = [0]
        q0_last = [0]
        for s in range(NSLOT):
            BS = 16 * s + 16
            W("gpsimd", DV, DV.n)
            W("gpsimd", AC, AC.n)
            mk_v = I("gpsimd", lambda e, s=s: e.dma_start(out=masks_sb[:], in_=masks[s]), LD, 16)

            if s == 0:
                W("sync", AC, AC.n)
                W("sync", DV, DV.n)
                pre_cast = []
                for tt in range(4):
                    buf = xq_f[tt % 2]
                    if tt >= 2:
                        W("sync", AC, pre_cast[tt - 2])
                    ld_v = I("sync", lambda e, b=buf, s=s, tt=tt: e.dma_start(
                        out=b[:], in_=xq[QB * s + 128 * tt: QB * s + 128 * (tt + 1), :]), TR, 16)
                    W("scalar", TR, ld_v)
                    W("scalar", PE, PE.n)
                    if tt == 0:
                        W("scalar", DV, DV.n)
                    v = I("scalar", lambda e, b=buf: e.activation(out=x1b[0][:], in_=b[:], func=AF.Copy), AC)
                    pre_cast.append(v)
                    W("tensor", AC, v)
                    W("tensor", DV, DV.n)
                    dst3 = xqT.ap().rearrange("p (d q) -> p d q", d=8)[:, :, 128 * tt:128 * (tt + 1)]
                    transpose_tile_to_xT(x1b[0], dst3)
                W("tensor", DV, DV.n)
                qlow = [y_sb[:, 0:QB], y_sb[:, QB:2 * QB]]
                for qlc in range(2):
                    for dc in range(8):
                        I("tensor", lambda e, qlc=qlc, dc=dc: e.matmul(
                            PS_M[:], wdqT[:, 256 * dc + 128 * qlc:256 * dc + 128 * (qlc + 1)], xqTc(dc),
                            start=(dc == 0), stop=(dc == 7), skip_group_check=True),
                            PE if dc == 7 else None)
                    W("scalar", PE, PE.n)
                    v = I("scalar", lambda e, qlc=qlc: e.activation(out=qlow[qlc], in_=PS_M[:], func=AF.Copy), AC)
                    W("tensor", AC, v)
                for dc in range(8):
                    for qlc in range(2):
                        I("tensor", lambda e, dc=dc, qlc=qlc: e.matmul(
                            PS_M[:], wuqT[qlc][:, 128 * dc:128 * (dc + 1)], qlow[qlc],
                            start=(qlc == 0), stop=(qlc == 1), skip_group_check=True),
                            PE if qlc == 1 else None)
                    W("scalar", PE, PE.n)
                    v = I("scalar", lambda e, dc=dc: e.activation(out=qTc(dc), in_=PS_M[:], func=AF.Copy), AC)
                    W("tensor", AC, v)
                qprep_pe_hold[0] = PE.n

            ac_ctxn, dv_den, ac_qlat = {}, {}, {}
            exp_last, pe_end = {}, {}
            c1_dv, c2_dv, b1_ac, b2_ac, b3_dv, b1_dv = {}, {}, {}, {}, {}, {}
            psm_free = [[(AC, AC.n)]]

            def use_psm():
                for sm, v in psm_free[0]:
                    W("tensor", sm, v)

            def set_psm(sm, v):
                psm_free[0] = [(sm, v)]

            def set_psm_multi(pairs):
                psm_free[0] = list(pairs)

            def prologue_q(h, lc, q0sum=True):
                p = h % 2
                cc, po = h // 2, 64 * (h % 2)
                bank = PS_SUM if (lc == 0 and q0sum) else PS_M
                if lc == 0 and q0sum:
                    W("tensor", DV, den_last[0])
                else:
                    use_psm()
                v = I("tensor", lambda e, cc=cc, po=po, lc=lc, bank=bank: e.matmul(
                    bank[:], keff[po:po + 64, 256 * cc + 128 * lc:256 * cc + 128 * (lc + 1)],
                    qT[po:po + 64, QB * cc:QB * (cc + 1)],
                    start=True, stop=True, skip_group_check=True), PE)
                W("scalar", PE, v)
                v = I("scalar", lambda e, p=p, lc=lc, bank=bank: e.activation(
                    out=qlat[p][lc][:], in_=bank[:], func=AF.Copy), AC)
                if lc == 0 and q0sum:
                    q0_last[0] = v
                else:
                    set_psm(AC, v)
                    if lc == 1:
                        ac_qlat[h] = v

            def prologue_c1(h):
                p = h % 2
                sm = 3 * p
                W("vector", AC, ac_qlat[h])
                VE(lambda e, p=p, sm=sm: e.tensor_reduce(smax[:, sm:sm + 1], qlat[p][0][:], AX.X,
                                                         ALU.max, apply_absolute_value=True))
                VE(lambda e, p=p, sm=sm: e.tensor_reduce(smax[:, sm + 1:sm + 2], qlat[p][1][:], AX.X,
                                                         ALU.max, apply_absolute_value=True))
                c1_dv[h] = VE(lambda e, sm=sm: e.tensor_max(smax[:, sm + 2:sm + 3], smax[:, sm:sm + 1],
                                                            smax[:, sm + 1:sm + 2]))

            def prologue_c2(h):
                p = h % 2
                sm = 3 * p
                sx = 5 + 2 * p
                use_psm()
                W("tensor", DV, c1_dv[h])
                v = I("tensor", lambda e, sm=sm: e.transpose(PS_M[0:1, 0:128], smax[:, sm + 2:sm + 3],
                                                             id_f[:]), PE)
                W("vector", PE, v)
                vr = VE(lambda e, sx=sx: e.tensor_reduce(sc1[0:1, sx:sx + 1], PS_M[0:1, 0:128], AX.X, ALU.max))
                set_psm(DV, vr)
                c2_dv[h] = VE(lambda e, sx=sx: e.tensor_mul(sc1[0:1, sx + 1:sx + 2], sc1[0:1, sx:sx + 1],
                                                            kfac[0:1, 0:1]))

            def prologue_c3(h):
                p = h % 2
                sx = 5 + 2 * p
                use_psm()
                W("tensor", DV, c2_dv[h])
                v = I("tensor", lambda e, sx=sx: e.matmul(PS_M[:, 0:1], ones_row_f[:],
                                                          sc1[0:1, sx + 1:sx + 2],
                                                          start=True, stop=True, skip_group_check=True), PE)
                W("scalar", PE, v)
                vv = SE(lambda e, h=h: e.activation(out=negc[:, h:h + 1], in_=PS_M[:, 0:1],
                                                    func=AF.Copy, scale=-1.0))
                set_psm(AC, vv)

            def epilogue_a(h, loop_pe):
                hp = h % 2
                W("scalar", PE, loop_pe)
                I("scalar", lambda e, hp=hp: e.activation(out=ctxn[hp][0][:], in_=PS_C[0][:], func=AF.Copy), AC)
                v = I("scalar", lambda e, hp=hp: e.activation(out=ctxn[hp][1][:], in_=PS_C[1][:], func=AF.Copy), AC)
                ac_ctxn[h] = v
                W("vector", PE, loop_pe)
                dv_den[h] = VE(lambda e, hp=hp: e.tensor_copy(den_sb[hp][:], PS_SUM[0:1, :]))
                den_last[0] = dv_den[h]

            def epilogue_b1(h):
                hp = h % 2
                rtmp = den_sb[1 - hp][:]
                W("scalar", DV, dv_den[h])
                SE(lambda e, hp=hp: e.activation(out=recip[hp][:], in_=den_sb[hp][:], func=AF.Ln))
                v0 = SE(lambda e, hp=hp: e.activation(out=recip[hp][:], in_=recip[hp][:],
                                                      func=AF.Exp, scale=-1.0))
                W("vector", AC, v0)
                VE(lambda e, hp=hp, rtmp=rtmp: e.tensor_mul(rtmp, den_sb[hp][:], recip[hp][:]))
                VE(lambda e, hp=hp, rtmp=rtmp: e.tensor_mul(rtmp, rtmp, recip[hp][:]))
                b1_dv[h] = VE(lambda e, hp=hp, rtmp=rtmp: e.scalar_tensor_tensor(
                    out=recip[hp][:], in0=recip[hp][:], scalar=2.0, in1=rtmp,
                    op0=ALU.mult, op1=ALU.subtract))

            def epilogue_b1b(h):
                hp = h % 2
                W("scalar", DV, b1_dv[h])
                b1_ac[h] = I("scalar", lambda e, hp=hp: e.activation(out=recip_b[hp][:], in_=recip[hp][:],
                                                                    func=AF.Copy), AC)

            def epilogue_b2(h):
                hp = h % 2
                use_psm()
                W("tensor", AC, b1_ac[h])
                v = I("tensor", lambda e, hp=hp: e.matmul(PS_M[:], ones_row_b[:], recip_b[hp][:],
                                                          start=True, stop=True, skip_group_check=True), PE)
                W("scalar", PE, v)
                b2_ac[h] = I("scalar", lambda e, hp=hp: e.activation(out=rbc[hp][:], in_=PS_M[:], func=AF.Copy), AC)
                set_psm(AC, b2_ac[h])

            def epilogue_b3(h):
                hp = h % 2
                W("vector", AC, b2_ac[h])
                I("vector", lambda e, hp=hp: e.tensor_mul(ctxn[hp][0][:], ctxn[hp][0][:], rbc[hp][:]), DV)
                b3_dv[h] = I("vector", lambda e, hp=hp: e.tensor_mul(ctxn[hp][1][:], ctxn[hp][1][:], rbc[hp][:]), DV)

            def epilogue_c(H):
                use_psm()
                W("tensor", DV, b3_dv[2 * H + 1])
                for hh in range(2):
                    hg = 2 * H + hh
                    for lc in range(2):
                        I("tensor", lambda e, hh=hh, hg=hg, lc=lc: e.matmul(
                            PS_M[64 * hh:64 * hh + 64, :],
                            vsb[lc][:, 64 * hg:64 * hg + 64], ctxn[hh][lc][:],
                            start=(lc == 0), stop=(lc == 1), skip_group_check=True),
                            PE if (hh == 1 and lc == 1) else None)
                W("scalar", PE, PE.n)
                v = I("scalar", lambda e, H=H: e.activation(
                    out=y_sb[:, QB * H:QB * (H + 1)], in_=PS_M[:], func=AF.Copy), AC)
                set_psm(AC, v)
                W("tensor", AC, v)

            slot_start_ac = AC.n
            slot_start_dv = DV.n
            stair_cast, stair_tr, stair_mm_pe = {}, {}, {}

            def stair_dma(h):
                T = 16 * (s + 1) + h
                if h >= 2:
                    W("sync", AC, stair_cast[h - 2])
                else:
                    W("sync", AC, slot_start_ac)
                return I("sync", lambda e, T=T, h=h: e.dma_start(
                    out=xq_f[h % 2][:], in_=xk[128 * T:128 * (T + 1), :]), TR, 16)

            def stair_cast_f(h, ld_v):
                W("scalar", TR, ld_v)
                if h >= 2:
                    W("scalar", TR, stair_tr[h - 2])
                else:
                    W("scalar", DV, slot_start_dv)
                stair_cast[h] = I("scalar", lambda e, h=h: e.activation(
                    out=x1b[h % 2][:], in_=xq_f[h % 2][:], func=AF.Copy), AC)

            def stair_trans(h):
                W("sync", AC, stair_cast[h])
                if h >= 1:
                    W("sync", PE, stair_mm_pe[h - 1])
                else:
                    W("sync", PE, last_stair_pe[0])
                for dc in range(8):
                    I("sync", lambda e, dc=dc, h=h: e.dma_start(
                        out=stairT[:, 128 * dc:128 * (dc + 1)],
                        in_=x1b[h % 2][:, 128 * dc:128 * (dc + 1)], transpose=True), TR, 16)
                stair_tr[h] = TR.n

            def stair_mm(h):
                T = 16 * (s + 1) + h
                W("tensor", TR, stair_tr[h])
                use_psm()
                for latc in range(2):
                    for dc in range(8):
                        I("tensor", lambda e, latc=latc, dc=dc: e.matmul(
                            PS_M[:, 128 * latc:128 * (latc + 1)],
                            wdkvT[:, 256 * dc + 128 * latc:256 * dc + 128 * (latc + 1)],
                            stairT[:, 128 * dc:128 * (dc + 1)],
                            start=(dc == 0), stop=(dc == 7), skip_group_check=True))
                for dc in range(8):
                    I("tensor", lambda e, dc=dc: e.matmul(
                        PS_M[:, 256:512], stairT[:, 128 * dc:128 * (dc + 1)],
                        wdkvT[:, 256 * dc:256 * (dc + 1)],
                        start=(dc == 0), stop=(dc == 7), skip_group_check=True),
                        PE if dc == 7 else None)
                stair_mm_pe[h] = PE.n
                last_stair_pe[0] = PE.n
                W("scalar", PE, PE.n)
                ac2 = None
                for latc in range(2):
                    ac2 = I("scalar", lambda e, latc=latc, T=T: e.activation(
                        out=ckvT[latc][:, 128 * T:128 * (T + 1)],
                        in_=PS_M[:, 128 * latc:128 * (latc + 1)], func=AF.Copy), AC)
                W("vector", AC, ac2)
                dv = I("vector", lambda e, T=T: e.tensor_copy(
                    ckvkm[:, 256 * T:256 * (T + 1)], PS_M[:, 256:512]), DV)
                set_psm(DV, dv)

            c0, c1 = 2048 * s, 2048 * (s + 1)
            VE(lambda e, c0=c0, c1=c1: e.tensor_reduce(smax[:, 0:1], ckvT[0][:, c0:c1], AX.X,
                                                       ALU.max, apply_absolute_value=True))
            VE(lambda e, c0=c0, c1=c1: e.tensor_reduce(smax[:, 1:2], ckvT[1][:, c0:c1], AX.X,
                                                       ALU.max, apply_absolute_value=True))
            v = VE(lambda e: e.tensor_max(smax[:, 2:3], smax[:, 0:1], smax[:, 1:2]))
            W("tensor", DV, v)
            use_psm()
            v = I("tensor", lambda e: e.transpose(PS_M[0:1, 0:128], smax[:, 2:3], id_f[:]), PE)
            W("vector", PE, v)
            VE(lambda e: e.tensor_reduce(sc1[0:1, 9:10], PS_M[0:1, 0:128], AX.X, ALU.max))
            VE(lambda e: e.tensor_max(sc1[0:1, 3:4], sc1[0:1, 3:4], sc1[0:1, 9:10]))
            v = VE(lambda e: e.tensor_scalar(out=sc1[0:1, 4:5], in0=sc1[0:1, 3:4],
                                             scalar1=32.0, scalar2=None, op0=ALU.mult))
            W("tensor", DV, v)
            v = I("tensor", lambda e: e.matmul(PS_M[:, 0:1], ones_row_f[:], sc1[0:1, 4:5],
                                               start=True, stop=True, skip_group_check=True), PE)
            W("scalar", PE, v)
            kf_v = SE(lambda e: e.activation(out=kfac[:], in_=PS_M[:, 0:1], func=AF.Copy))
            set_psm(AC, kf_v)

            st_ld_next = [0]
            if s < 3:
                st_ld_next[0] = stair_dma(0)
            prologue_q(0, 0, q0sum=False)
            prologue_q(0, 1)
            prologue_c1(0)
            prologue_c2(0)
            prologue_c3(0)

            for h in range(NH):
                hp = h % 2
                p = h % 2
                exp_ac, msk_dv, s_pe, pacc_dv, sum_pe, ctx_pe = {}, {}, {}, {}, {}, {}

                def plan_S(kc, h=h, p=p, exp_ac=exp_ac, s_pe=s_pe):
                    par = kc % 4
                    if kc >= 4:
                        W("tensor", AC, exp_ac[kc - 4])
                    elif h > 0:
                        W("tensor", AC, exp_last[h - 1][kc % 4])
                    if kc == 0:
                        W("tensor", AC, ac_qlat.get(h, 0))
                    for lc in range(2):
                        I("tensor", lambda e, kc=kc, lc=lc, par=par, p=p: e.matmul(
                            PS_S[par], ckvT[lc][:, 128 * kc:128 * (kc + 1)], qlat[p][lc][:],
                            start=(lc == 0), stop=(lc == 1), skip_group_check=True),
                            PE if lc == 1 else None)
                    s_pe[kc] = PE.n

                def plan_exp(kc, h=h, s=s, exp_ac=exp_ac, msk_dv=msk_dv,
                             s_pe=s_pe, pacc_dv=pacc_dv, ctx_pe=ctx_pe):
                    par = kc % 4
                    if kc >= 4:
                        W("scalar", PE, ctx_pe[kc - 4])
                        W("scalar", DV, pacc_dv[kc - 4])
                    else:
                        W("scalar", PE, s_pe[kc])
                        if h > 0:
                            W("scalar", DV, dv_den[h - 1])
                    v = I("scalar", lambda e, par=par, h=h: e.activation(
                        out=PT[par][:], in_=PS_S[par], func=AF.Exp, scale=0.125,
                        bias=negc[:, h:h + 1]), AC)
                    exp_ac[kc] = v
                    W("vector", AC, v)
                    if kc >= 16 * s:
                        w_i = kc - 16 * s
                        msk_dv[kc] = I("vector", lambda e, par=par, w_i=w_i: e.tensor_mul(
                            PT[par][:], PT[par][:], masks_sb[:, 512 * w_i:512 * (w_i + 1)]), DV)
                    g = kc % 8
                    pb = (kc // 8) % 2
                    if g == 0:
                        gi = kc // 8
                        if gi >= 2:
                            W("vector", PE, sum_pe[gi - 2])
                        elif h > 0:
                            W("vector", PE, pe_end[h - 1])
                        pacc_dv[kc] = I("vector", lambda e, par=par, pb=pb: e.tensor_copy(
                            Pacc[pb][:], PT[par][:]), DV)
                    else:
                        pacc_dv[kc] = I("vector", lambda e, par=par, pb=pb: e.tensor_add(
                            Pacc[pb][:], Pacc[pb][:], PT[par][:]), DV)

                def plan_ctx(kc, h=h, BS=BS, exp_ac=exp_ac, msk_dv=msk_dv,
                             pacc_dv=pacc_dv, sum_pe=sum_pe, ctx_pe=ctx_pe):
                    par = kc % 4
                    if kc in msk_dv:
                        W("tensor", DV, msk_dv[kc])
                    else:
                        W("tensor", AC, exp_ac[kc])
                    first, last = (kc == 0), (kc == BS - 1)
                    if first and h > 0:
                        W("tensor", AC, ac_ctxn[h - 1])
                        W("tensor", DV, dv_den[h - 1])
                    for lc in range(2):
                        I("tensor", lambda e, kc=kc, lc=lc, par=par: e.matmul(
                            PS_C[lc][:], ckvkm[:, 256 * kc + 128 * lc:256 * kc + 128 * (lc + 1)],
                            PT[par][:], start=first, stop=last, skip_group_check=True),
                            PE if lc == 1 else None)
                    ctx_pe[kc] = PE.n
                    if kc % 8 == 7:
                        pb = (kc // 8) % 2
                        W("tensor", DV, pacc_dv[kc])
                        if kc == 7:
                            W("tensor", AC, q0_last[0])
                        I("tensor", lambda e, pb=pb, kc=kc, BS=BS: e.matmul(
                            PS_SUM[0:1, :], ones_col[:], Pacc[pb][:],
                            start=(kc == 7), stop=(kc == BS - 1), skip_group_check=True), PE)
                        sum_pe[kc // 8] = PE.n

                if h == 0:
                    W("scalar", LD, mk_v)
                for k0 in range(min(4, BS)):
                    plan_S(k0)
                    plan_exp(k0)
                for kc in range(BS):
                    if kc + 4 < BS:
                        plan_S(kc + 4)
                    plan_ctx(kc)
                    if kc + 4 < BS:
                        plan_exp(kc + 4)
                    if kc == 1 and h > 0:
                        epilogue_b1(h - 1)
                    elif kc == 2 and s < 3:
                        stair_cast_f(h, st_ld_next[0])
                    elif kc == 6 and s < 3 and h + 1 < NH:
                        st_ld_next[0] = stair_dma(h + 1)
                    elif kc == 3 and h + 1 < NH:
                        prologue_q(h + 1, 0, q0sum=False)
                    elif kc == 4:
                        if s < 3:
                            stair_trans(h)
                        if h > 0:
                            epilogue_b1b(h - 1)
                    elif kc == 5 and h + 1 < NH:
                        prologue_q(h + 1, 1)
                        prologue_c1(h + 1)
                    elif kc == 9 and h + 1 < NH:
                        prologue_c2(h + 1)
                    elif kc == 8 and h > 0:
                        epilogue_b2(h - 1)
                    elif kc == 10 and h > 0:
                        epilogue_b3(h - 1)
                    elif kc == 15 and h >= 2 and h % 2 == 0:
                        epilogue_c((h - 2) // 2)
                    elif kc == 13 and s < 3:
                        stair_mm(h)
                    elif kc == 14 and h + 1 < NH:
                        prologue_c3(h + 1)
                exp_last[h] = {kc % 4: exp_ac[kc] for kc in (BS - 4, BS - 3, BS - 2, BS - 1)}
                pe_end[h] = PE.n
                epilogue_a(h, PE.n)
            epilogue_b1(NH - 1)
            epilogue_b1b(NH - 1)
            epilogue_b2(NH - 1)
            epilogue_b3(NH - 1)
            epilogue_c(NH // 2 - 1)
            ysb_ac = AC.n

            def plan_ffn(tt):
                par = tt % 2
                if tt >= 1:
                    W("tensor", DV, r2_dv[tt - 1])
                else:
                    W("tensor", AC, ac_ctxn[NH - 1])
                    W("tensor", DV, dv_den[NH - 1])
                W("tensor", AC, uT_ac[tt])
                for fbk in range(2):
                    bank = PS_C[fbk]
                    for dc in range(8):
                        I("tensor", lambda e, dc=dc, fbk=fbk, bank=bank: e.matmul(
                            bank[:], x1T[:, 128 * dc:128 * (dc + 1)], wq[dc][:, QB * fbk:QB * (fbk + 1)],
                            start=(dc == 0), stop=False, skip_group_check=True))
                    I("tensor", lambda e, fbk=fbk, bank=bank: e.matmul(
                        bank[:], uT[:], b_sb[:, QB * fbk:QB * (fbk + 1)],
                        start=False, stop=False, skip_group_check=True))
                    I("tensor", lambda e, fbk=fbk, bank=bank: e.matmul(
                        bank[:], ones_row_b[:], bias_sb[0:1, QB * fbk:QB * (fbk + 1)],
                        start=False, stop=True, skip_group_check=True), PE)
                    W("vector", PE, PE.n)
                    I("vector", lambda e, fbk=fbk, par=par, bank=bank: e.tensor_add(
                        r2[par][:, QB * fbk:QB * (fbk + 1)], x1b[par][:, QB * fbk:QB * (fbk + 1)],
                        bank[:]), DV)
                return DV.n

            def plan_ln(src, scratch, b0):
                sq = I("scalar", lambda e, src=src, scratch=scratch: e.activation(
                    out=scratch, in_=src, func=AF.Square, accum_out=stats[:, b0:b0 + 1]), AC)
                VE(lambda e, src=src: e.tensor_reduce(stats[:, b0 + 1:b0 + 2], src, AX.X, ALU.add))
                W("vector", AC, sq)
                VE(lambda e: e.tensor_scalar(out=stats[:, b0 + 4:b0 + 5], in0=stats[:, b0 + 1:b0 + 2],
                                             scalar1=stats[:, b0 + 1:b0 + 2], scalar2=1.0 / (1024.0 * 1024.0),
                                             op0=ALU.mult, op1=ALU.mult))
                v = VE(lambda e: e.scalar_tensor_tensor(
                    out=stats[:, b0 + 5:b0 + 6], in0=stats[:, b0:b0 + 1], scalar=1.0 / 1024,
                    in1=stats[:, b0 + 4:b0 + 5], op0=ALU.mult, op1=ALU.subtract))
                W("scalar", DV, v)
                SE(lambda e: e.activation(out=stats[:, b0 + 6:b0 + 7], in_=stats[:, b0 + 5:b0 + 6],
                                          func=AF.Ln, bias=eps_t[:]))
                v = SE(lambda e: e.activation(out=stats[:, b0 + 7:b0 + 8], in_=stats[:, b0 + 6:b0 + 7],
                                              func=AF.Exp, scale=-0.5))
                W("vector", AC, v)
                v = VE(lambda e: e.scalar_tensor_tensor(
                    out=stats[:, b0 + 9:b0 + 10], in0=stats[:, b0 + 1:b0 + 2], scalar=-1.0 / 1024,
                    in1=stats[:, b0 + 7:b0 + 8], op0=ALU.mult, op1=ALU.mult))
                W("scalar", DV, v)

            ln1_ac, x1T_dv, r2_dv, uT_ac = {}, {}, {}, {}
            r1_dv = {}
            prep_cast2, prep_tr2, pld_of = {}, {}, {}
            pa_dv0 = DV.n
            for tt in range(5):
                par = tt % 2
                if tt >= 1:
                    pv = tt - 1
                    parv = pv % 2
                    W("tensor", AC, ln1_ac[pv])
                    if pv >= 1:
                        W("tensor", DV, r1_dv[pv])
                    for dc in range(8):
                        I("tensor", lambda e, dc=dc, parv=parv: e.transpose(
                            PB0[:, 128 * dc:128 * (dc + 1)], x1b[parv][:, 128 * dc:128 * (dc + 1)],
                            id_b[:]), PE if dc == 7 else None)
                    W("vector", PE, PE.n)
                    x1T_dv[pv] = I("vector", lambda e: e.tensor_copy(x1T[:], PB0[:]), DV)
                if tt < 4:
                    if tt >= 2:
                        W("sync", AC, ln1_ac[tt - 2])
                        W("sync", DV, x1T_dv.get(tt - 2, 0))
                    elif s < 3:
                        W("sync", AC, stair_cast[14 + tt])
                    ld_v = I("sync", lambda e, s=s, tt=tt, par=par: e.dma_start(
                        out=xq_f[par][:], in_=xq[QB * s + 128 * tt: QB * s + 128 * (tt + 1), :]), TR, 16)
                    if s + 1 < NSLOT:
                        def stg_of(k):
                            return masks_sb.ap().bitcast(F32)[:, 1024 * (k % 2):1024 * (k % 2 + 1)]

                        def pld_issue(k):
                            if k < 2:
                                W("sync", DV, pa_dv0)
                            else:
                                W("sync", AC, prep_cast2[k - 2])
                            pld_of[k] = I("sync", lambda e, stg=stg_of(k), s=s, k=k: e.dma_start(
                                out=stg, in_=xq[QB * (s + 1) + 128 * k: QB * (s + 1) + 128 * (k + 1), :]),
                                (TGA, TGB)[k % 2], 16)

                        if tt == 0:
                            pld_issue(0)
                            pld_issue(1)
                        W("scalar", (TGA, TGB)[tt % 2], pld_of[tt])
                        if tt == 0:
                            W("scalar", PE, last_stair_pe[0])
                        else:
                            W("scalar", TR, prep_tr2[tt - 1])
                        prep_cast2[tt] = I("scalar", lambda e, stg=stg_of(tt): e.activation(
                            out=stairT[:], in_=stg, func=AF.Copy), AC)
                        if tt + 2 < 4:
                            pld_issue(tt + 2)
                        W("sync", AC, prep_cast2[tt])
                        if tt == 0:
                            W("sync", PE, qprep_pe_hold[0])
                        for dc in range(8):
                            I("sync", lambda e, dc=dc, tt=tt: e.dma_start(
                                out=xqT[:, QB * dc + 128 * tt:QB * dc + 128 * (tt + 1)],
                                in_=stairT[:, 128 * dc:128 * (dc + 1)], transpose=True), TR, 16)
                        prep_tr2[tt] = TR.n
                    if tt == 0:
                        W("tensor", AC, ysb_ac)
                    else:
                        W("tensor", DV, x1T_dv[tt - 1])
                    for Hc in range(8):
                        I("tensor", lambda e, Hc=Hc, tt=tt: e.transpose(
                            PB0[:, 128 * Hc:128 * (Hc + 1)],
                            y_sb[:, QB * Hc + 128 * tt: QB * Hc + 128 * (tt + 1)], id_b[:]),
                            PE if Hc == 7 else None)
                    ytr_pe = PE.n
                if tt >= 1:
                    pv = tt - 1
                    use_psm()
                    W("tensor", DV, x1T_dv[pv])
                    for dc in range(8):
                        I("tensor", lambda e, dc=dc: e.matmul(
                            PS_M[0:16, 0:128], a_sb[:, 16 * dc:16 * (dc + 1)],
                            x1T[:, 128 * dc:128 * (dc + 1)],
                            start=(dc == 0), stop=(dc == 7), skip_group_check=True),
                            PE if dc == 7 else None)
                    W("scalar", PE, PE.n)
                    uT_ac[pv] = I("scalar", lambda e: e.activation(
                        out=uT[:], in_=PS_M[0:16, 0:128], func=AF.Copy), AC)
                    set_psm(AC, uT_ac[pv])
                if tt < 4:
                    W("vector", PE, ytr_pe)
                    W("vector", TR, ld_v)
                    r1_dv[tt] = I("vector", lambda e, par=par: e.tensor_add(
                        xq_f[par][:], xq_f[par][:], PB0[:]), DV)
                    W("scalar", DV, r1_dv[tt])
                    if tt >= 2:
                        W("scalar", DV, r2_dv[tt - 2])
                    elif s < 3:
                        W("scalar", TR, stair_tr[14 + tt])
                    plan_ln(xq_f[par][:], r2[par][:], 0)
                    v = I("scalar", lambda e, par=par: e.activation(
                        out=x1b[par][:], in_=xq_f[par][:], func=AF.Identity,
                        scale=stats[:, 7:8], bias=stats[:, 9:10]), AC)
                    ln1_ac[tt] = v
                if tt >= 1:
                    r2_dv[pv] = plan_ffn(pv)
                    W("scalar", DV, r2_dv[pv])
                    if n_out_dma >= 2:
                        W("scalar", OT, 16 * (n_out_dma - 1))
                    plan_ln(r2[parv][:], out_st[parv][:], 12)
                    ov = I("scalar", lambda e, parv=parv: e.activation(
                        out=out_st[parv][:], in_=r2[parv][:], func=AF.Identity,
                        scale=stats[:, 19:20], bias=stats[:, 21:22]), AC)
                    W("gpsimd", AC, ov)
                    I("gpsimd", lambda e, s=s, pv=pv, parv=parv: e.dma_start(
                        out=out_hbm[QB * s + 128 * pv: QB * s + 128 * (pv + 1), :],
                        in_=out_st[parv][:]), OT, 16)
                    n_out_dma += 1
            if s + 1 < NSLOT:
                W("tensor", TR, prep_tr2[3])
                qlow2 = [Pacc[0][:], Pacc[1][:]]
                for qlc in range(2):
                    use_psm()
                    for dc in range(8):
                        I("tensor", lambda e, qlc=qlc, dc=dc: e.matmul(
                            PS_M[:], wdqT[:, 256 * dc + 128 * qlc:256 * dc + 128 * (qlc + 1)], xqTc(dc),
                            start=(dc == 0), stop=(dc == 7), skip_group_check=True),
                            PE if dc == 7 else None)
                    W("scalar", PE, PE.n)
                    v = I("scalar", lambda e, qlc=qlc: e.activation(out=qlow2[qlc], in_=PS_M[:], func=AF.Copy), AC)
                    set_psm(AC, v)
                for dc in range(8):
                    use_psm()
                    for qlc in range(2):
                        I("tensor", lambda e, dc=dc, qlc=qlc: e.matmul(
                            PS_M[:], wuqT[qlc][:, 128 * dc:128 * (dc + 1)], qlow2[qlc],
                            start=(qlc == 0), stop=(qlc == 1), skip_group_check=True),
                            PE if qlc == 1 else None)
                    W("scalar", PE, PE.n)
                    v = I("scalar", lambda e, dc=dc: e.activation(out=qTc(dc), in_=PS_M[:], func=AF.Copy), AC)
                    set_psm(AC, v)
                qprep_pe_hold[0] = PE.n

        W("gpsimd", OT, OT.n)

        blk = E(nc.Block())

        @blk.tensor
        def _(eng):
            for fn in plan["tensor"]:
                fn(eng)

        @blk.scalar
        def _(eng):
            for fn in plan["scalar"]:
                fn(eng)

        @blk.vector
        def _(eng):
            for fn in plan["vector"]:
                fn(eng)

        @blk.sync
        def _(eng):
            for fn in plan["sync"]:
                fn(eng)

        @blk.gpsimd
        def _(eng):
            for fn in plan["gpsimd"]:
                fn(eng)

    return nc


def _host_masks(c):
    m = np.zeros((NSLOT, 128, 16 * 512), dtype=np.float32)
    tri = (np.arange(128)[:, None] <= np.arange(128)[None, :]).astype(np.float32)
    for s in range(NSLOT):
        for w in range(16):
            for j in range(4):
                mt = 4 * c + j
                if w < mt:
                    m[s, :, 512 * w + 128 * j: 512 * w + 128 * (j + 1)] = 1.0
                elif w == mt:
                    m[s, :, 512 * w + 128 * j: 512 * w + 128 * (j + 1)] = tri
    return m.astype(BF)


def _gather_rows(c):
    idx = []
    for s in range(NSLOT):
        for j in range(4):
            mt = 16 * s + 4 * c + j
            idx.append(np.arange(128 * mt, 128 * (mt + 1)))
    return np.concatenate(idx)


def kernel(L0_attn_x, L0_attn_W_dq, L0_attn_W_uq, L0_attn_W_dkv, L0_attn_W_uk,
           L0_attn_W_uv, L0_attn_W_o, L0_ffn_kernel, L0_ffn_bias, L0_ffn_a, L0_ffn_b):
    x = np.ascontiguousarray(np.asarray(L0_attn_x, dtype=np.float32))
    common = {
        "w_dq": np.ascontiguousarray(np.asarray(L0_attn_W_dq, np.float32)),
        "w_uq": np.ascontiguousarray(np.asarray(L0_attn_W_uq, np.float32)),
        "w_dkv": np.ascontiguousarray(np.asarray(L0_attn_W_dkv, np.float32)),
        "w_uk": np.ascontiguousarray(np.asarray(L0_attn_W_uk, np.float32)),
        "w_uv": np.ascontiguousarray(np.asarray(L0_attn_W_uv, np.float32)),
        "w_o": np.ascontiguousarray(np.asarray(L0_attn_W_o, np.float32)),
        "fk": np.ascontiguousarray(np.asarray(L0_ffn_kernel, np.float32)),
        "fbias": np.ascontiguousarray(np.asarray(L0_ffn_bias, np.float32).reshape(1, C)),
        "fa": np.ascontiguousarray(np.asarray(L0_ffn_a, np.float32)),
        "fb": np.ascontiguousarray(np.asarray(L0_ffn_b, np.float32)),
        "ident_b": np.eye(128, dtype=BF),
        "ident_f": np.eye(128, dtype=np.float32),
    }
    in_maps = []
    gathers = []
    for j in range(8):
        b, c = j // 4, j % 4
        rows = _gather_rows(c)
        gathers.append((b, rows))
        in_maps.append({
            **common,
            "xk": np.ascontiguousarray(x[b]),
            "xq": np.ascontiguousarray(x[b][rows]),
            "masks": _host_masks(c),
        })
    nc = build_nc()
    res = run_bass_kernel_spmd(nc, in_maps, core_ids=list(range(8)))
    out = np.empty((B, T, C), dtype=np.float32)
    for j in range(8):
        b, rows = gathers[j]
        out[b][rows] = np.asarray(res.results[j]["out"], dtype=np.float32)
    return out


# version 46 (final)
# speedup vs baseline: 1.0214x; 1.0214x over previous
import numpy as np
import ml_dtypes
from contextlib import ExitStack

import concourse.bass as bass
import concourse.mybir as mybir
from concourse.bass_utils import run_bass_kernel_spmd

F32 = mybir.dt.float32
BF16 = mybir.dt.bfloat16
AF = mybir.ActivationFunctionType
ALU = mybir.AluOpType
AX = mybir.AxisListType

B, T, C = 2, 8192, 1024
NH, HS, KVL, QL = 16, 64, 256, 256
NSLOT, QB = 4, 512
NKG = T // 512
NKC = T // 128
EPS = 1e-5
BF = ml_dtypes.bfloat16
I32 = mybir.dt.int32


def build_nc():
    nc = bass.Bass("TRN2", target_bir_lowering=False)

    xk = nc.dram_tensor("xk", [T, C], F32, kind="ExternalInput").ap()
    xq = nc.dram_tensor("xq", [NSLOT * QB, C], F32, kind="ExternalInput").ap()
    masks = nc.dram_tensor("masks", [NSLOT, 128, 16 * 512], BF16, kind="ExternalInput").ap()
    w_dq = nc.dram_tensor("w_dq", [QL, C], F32, kind="ExternalInput").ap()
    w_uq = nc.dram_tensor("w_uq", [C, QL], F32, kind="ExternalInput").ap()
    w_dkv = nc.dram_tensor("w_dkv", [KVL, C], F32, kind="ExternalInput").ap()
    w_uk = nc.dram_tensor("w_uk", [C, KVL], F32, kind="ExternalInput").ap()
    w_uv = nc.dram_tensor("w_uv", [C, KVL], F32, kind="ExternalInput").ap()
    w_o = nc.dram_tensor("w_o", [C, C], F32, kind="ExternalInput").ap()
    fk = nc.dram_tensor("fk", [C, C], F32, kind="ExternalInput").ap()
    fbias = nc.dram_tensor("fbias", [1, C], F32, kind="ExternalInput").ap()
    fa = nc.dram_tensor("fa", [C, 16], F32, kind="ExternalInput").ap()
    fb = nc.dram_tensor("fb", [16, C], F32, kind="ExternalInput").ap()
    ident_b = nc.dram_tensor("ident_b", [128, 128], BF16, kind="ExternalInput").ap()
    ident_f = nc.dram_tensor("ident_f", [128, 128], F32, kind="ExternalInput").ap()
    out_hbm = nc.dram_tensor("out", [NSLOT * QB, C], F32, kind="ExternalOutput").ap()

    with ExitStack() as stack:
        E = stack.enter_context

        ckvT = [E(nc.sbuf_tensor(f"ckvT{i}", [128, T], BF16)) for i in range(2)]
        ckvkm = E(nc.sbuf_tensor("ckvkm", [128, NKC * 256], BF16))
        masks_sb = E(nc.sbuf_tensor("masks_sb", [128, 16 * 512], BF16))
        xq_f = [E(nc.sbuf_tensor(f"xq_f{i}", [128, C], F32)) for i in range(2)]
        xqT = E(nc.sbuf_tensor("xqT", [128, 8 * QB], BF16))
        qT = E(nc.sbuf_tensor("qT", [128, 8 * QB], BF16))
        qlat = [[E(nc.sbuf_tensor(f"qlat{p}{i}", [128, QB], BF16)) for i in range(2)]
                for p in range(2)]
        PT = [E(nc.sbuf_tensor(f"PT{i}", [128, QB], BF16)) for i in range(4)]
        Pacc = [E(nc.sbuf_tensor(f"Pacc{i}", [128, QB], BF16)) for i in range(2)]
        ctxn = [[E(nc.sbuf_tensor(f"ctxn{i}{l}", [128, QB], BF16)) for l in range(2)]
                for i in range(2)]
        y_sb = E(nc.sbuf_tensor("y_sb", [128, 8 * QB], BF16))
        x1b = [E(nc.sbuf_tensor(f"x1b{i}", [128, C], BF16)) for i in range(2)]
        x1T = E(nc.sbuf_tensor("x1T", [128, C], BF16))
        r2 = [E(nc.sbuf_tensor(f"r2_{i}", [128, C], F32)) for i in range(2)]
        out_st = [E(nc.sbuf_tensor(f"out_st{i}", [128, C], F32)) for i in range(2)]
        wdqT = E(nc.sbuf_tensor("wdqT", [128, 8 * QL], BF16))
        wuqT = [E(nc.sbuf_tensor(f"wuqT{i}", [128, C], BF16)) for i in range(2)]
        wdkvT = E(nc.sbuf_tensor("wdkvT", [128, 8 * KVL], BF16))
        keff = E(nc.sbuf_tensor("keff", [128, 8 * KVL], BF16))
        vsb = [E(nc.sbuf_tensor(f"vsb{i}", [128, C], BF16)) for i in range(2)]
        wq = [E(nc.sbuf_tensor(f"wq{i}", [128, C], BF16)) for i in range(8)]
        a_sb = E(nc.sbuf_tensor("a_sb", [128, 8 * 16], BF16))
        b_sb = E(nc.sbuf_tensor("b_sb", [16, C], BF16))
        bias_sb = E(nc.sbuf_tensor("bias_sb", [1, C], BF16))
        id_b = E(nc.sbuf_tensor("id_b", [128, 128], BF16))
        id_f = E(nc.sbuf_tensor("id_f", [128, 128], F32))
        ones_col = E(nc.sbuf_tensor("ones_col", [128, 1], BF16))
        ones_row_b = E(nc.sbuf_tensor("ones_row_b", [1, 128], BF16))
        ones_row_f = E(nc.sbuf_tensor("ones_row_f", [1, 128], F32))
        uT = E(nc.sbuf_tensor("uT", [16, 128], BF16))
        stats = E(nc.sbuf_tensor("stats", [128, 24], F32))
        smax = E(nc.sbuf_tensor("smax", [128, 16], F32))
        negc = E(nc.sbuf_tensor("negc", [128, 16], F32))
        kfac = E(nc.sbuf_tensor("kfac", [128, 1], F32))
        eps_t = E(nc.sbuf_tensor("eps_t", [128, 1], F32))
        m7 = E(nc.sbuf_tensor("m7", [128, 1], F32))
        r7 = E(nc.sbuf_tensor("r7", [128, 1], F32))
        sc1 = E(nc.sbuf_tensor("sc1", [1, 12], F32))
        recip = [E(nc.sbuf_tensor(f"recip{p}", [1, QB], F32)) for p in range(2)]
        recip_b = [E(nc.sbuf_tensor(f"recip_b{p}", [1, QB], BF16)) for p in range(2)]
        rbc = [E(nc.sbuf_tensor(f"rbc{p}", [128, QB], F32)) for p in range(2)]
        den_sb = [E(nc.sbuf_tensor(f"den{p}", [1, QB], F32)) for p in range(2)]
        stairT = E(nc.sbuf_tensor("stairT", [128, C], BF16))

        def xqTc(dc):
            return xqT[:, QB * dc:QB * (dc + 1)]

        def qTc(dc):
            return qT[:, QB * dc:QB * (dc + 1)]

        PS_S3 = [E(nc.psum_tensor(f"ps_s{i}", [128, QB], F32)) for i in range(3)]
        PS_C = [E(nc.psum_tensor(f"ps_c{i}", [128, QB], F32)) for i in range(2)]
        PS_SUM = E(nc.psum_tensor("ps_sum", [128, QB], F32))
        PS_M = E(nc.psum_tensor("ps_m", [128, QB], F32))
        PB0 = E(nc.psum_tensor("pb0", [128, 1024], BF16))
        PS_S = [t[:] for t in PS_S3] + [PB0.ap().bitcast(F32)]

        class Sem:
            def __init__(self, name):
                self.h = E(nc.semaphore(name))
                self.n = 0

        LD = Sem("LD")
        SA = Sem("SA")
        SB = Sem("SB")
        LQA = Sem("LQA")
        LQB = Sem("LQB")
        TGA = Sem("TGA")
        TGB = Sem("TGB")
        TR = Sem("TR")
        PE = Sem("PE")
        AC = Sem("AC")
        DV = Sem("DV")
        OT = Sem("OT")

        plan = {e: [] for e in ("tensor", "scalar", "vector", "sync", "gpsimd")}

        def P(e, fn):
            plan[e].append(fn)

        def W(e, sem, val):
            if val > 0:
                P(e, lambda eng, sh=sem.h, v=val: eng.wait_ge(sh, v))

        def barrier():
            vals = [(sm, sm.n) for sm in (LD, SA, SB, LQA, LQB, TGA, TGB, TR, PE, AC, DV, OT)]
            for e in plan:
                for sm, v in vals:
                    W(e, sm, v)

        def I(e, fn, sem=None, k=1):
            if sem is None:
                P(e, lambda eng, f=fn: f(eng))
                return None
            sem.n += k
            P(e, lambda eng, f=fn, sh=sem.h, kk=k: f(eng).then_inc(sh, kk))
            return sem.n

        def VE(fn):
            v = I("vector", fn, DV)
            W("vector", DV, v)
            return v

        def SE(fn):
            v = I("scalar", fn, AC)
            W("scalar", AC, v)
            return v

        def transpose_tile_to_xT(src_bf, dst_view3):
            for dc in range(8):
                I("tensor", lambda e, dc=dc: e.transpose(
                    PB0[:, 128 * dc:128 * (dc + 1)], src_bf[:, 128 * dc:128 * (dc + 1)], id_b[:]),
                    PE if dc == 7 else None)
            pe_v = PE.n
            W("vector", PE, pe_v)
            dv_v = I("vector", lambda e, d=dst_view3: e.tensor_copy(
                d, PB0.ap().rearrange("p (d q) -> p d q", d=8)), DV)
            return pe_v, dv_v

        wdq_nat = [masks_sb[:, i * C:(i + 1) * C] for i in range(2)]
        wuq_nat8 = [masks_sb[:, 2 * C + i * 256: 2 * C + (i + 1) * 256] for i in range(8)]
        wuk_nat = [masks_sb[:, 4 * C + i * 256: 4 * C + (i + 1) * 256] for i in range(8)]
        wuv_nat = [ckvkm[:, 8 * C + i * 256: 8 * C + (i + 1) * 256] for i in range(8)]
        wdkv_natb = [y_sb[:, 0:C], y_sb[:, C:2 * C]]
        ke1 = [y_sb[:, 2 * C + i * 256: 2 * C + (i + 1) * 256] for i in range(2)]
        stair_ph0 = [ckvkm[:, 12 * C:13 * C], ckvkm[:, 13 * C:14 * C]]
        qint = ckvkm[:, 14 * C:16 * C].bitcast(I32)
        x1f_s = ckvkm[:, 4 * C:6 * C].bitcast(F32)

        I("gpsimd", lambda e: e.dma_start(out=id_b[:], in_=ident_b), LD, 16)
        I("gpsimd", lambda e: e.dma_start(out=id_f[:], in_=ident_f), LD, 16)
        I("vector", lambda e: e.memset(ones_col[:], 1.0))
        I("vector", lambda e: e.memset(ones_row_b[:], 1.0))
        I("vector", lambda e: e.memset(ones_row_f[:], 1.0))
        I("vector", lambda e: e.memset(sc1[0:1, 3:4], 0.0))
        I("vector", lambda e: e.memset(eps_t[:], EPS), DV)

        def batch8_transpose(srcs, dst3):
            W("tensor", DV, DV.n)
            for dc in range(8):
                I("tensor", lambda e, s2=srcs[dc], dc=dc: e.transpose(
                    PB0[:, 128 * dc:128 * (dc + 1)], s2, id_b[:]), PE if dc == 7 else None)
            W("vector", PE, PE.n)
            v = I("vector", lambda e, d=dst3: e.tensor_copy(
                d, PB0.ap().rearrange("p (d q) -> p d q", d=8)), DV)
            W("tensor", DV, v)
            return v

        stage_jobs = []
        stage_jobs += [("wdq%d" % i, w_dq[128 * i:128 * (i + 1), :], C, wdq_nat[i]) for i in range(2)]
        stage_jobs += [("wdkv%d" % i, w_dkv[128 * i:128 * (i + 1), :], C, wdkv_natb[i]) for i in range(2)]
        stage_jobs += [("wuq%d" % i, w_uq[128 * i:128 * (i + 1), :], QL, wuq_nat8[i]) for i in range(8)]
        stage_jobs += [("wuk%d" % i, w_uk[128 * i:128 * (i + 1), :], KVL, wuk_nat[i]) for i in range(8)]
        stage_jobs += [("wuv%d" % i, w_uv[128 * i:128 * (i + 1), :], KVL, wuv_nat[i]) for i in range(8)]
        stage_jobs += [("fa%d" % i, fa[128 * i:128 * (i + 1), :], 16, a_sb[:, 16 * i:16 * (i + 1)]) for i in range(8)]
        wo_b = [masks_sb[:, i * C:(i + 1) * C] for i in range(8)]
        wo_jobs = [("wo%d" % i, w_o[128 * i:128 * (i + 1), :], C, wo_b[i]) for i in range(8)]

        cast_of = {}
        stage_ctr = {}
        stage_prev = []

        def plan_stage(jobs, k=1, gate=None):
            key = id(jobs)
            stage_ctr.setdefault(key, 0)
            for _ in range(k):
                if stage_ctr[key] >= len(jobs):
                    return
                name, src2, ncol, dst = jobs[stage_ctr[key]]
                stage_ctr[key] += 1
                gi = len(stage_prev)
                buf = r2[gi % 2]
                if gi >= 2:
                    W("gpsimd", AC, cast_of[stage_prev[gi - 2]])
                if gate is not None:
                    W("gpsimd", PE, gate)
                sm = (SA, SB)[gi % 2]
                ld_v = I("gpsimd", lambda e, b=buf, s2=src2, n=ncol: e.dma_start(out=b[:, 0:n], in_=s2), sm, 16)
                W("scalar", sm, ld_v)
                cast_of[name] = I("scalar", lambda e, b=buf, d=dst, n=ncol: e.activation(
                    out=d, in_=b[:, 0:n], func=AF.Copy), AC)
                stage_prev.append(name)

        x_cast, x_trb, x_mm = {}, {}, {}

        def x_piece(t):
            if t >= 2:
                W("sync", AC, x_cast[t - 2])
            ld_v = I("sync", lambda e, t=t: e.dma_start(
                out=xq_f2[t % 2][:], in_=xk[128 * t:128 * (t + 1), :]), TR, 16)
            W("scalar", TR, ld_v)
            if t >= 2:
                W("scalar", PE, x_trb[t - 2])
            x_cast[t] = I("scalar", lambda e, t=t: e.activation(
                out=x1b[t % 2][:], in_=xq_f2[t % 2][:], func=AF.Copy), AC)
            W("tensor", AC, x_cast[t])
            W("tensor", DV, DV.n)
            for dc in range(8):
                I("tensor", lambda e, dc=dc, t=t: e.transpose(
                    PB0[:, 128 * dc:128 * (dc + 1)], x1b[t % 2][:, 128 * dc:128 * (dc + 1)],
                    id_b[:]), PE if dc == 7 else None)
            x_trb[t] = PE.n
            W("vector", PE, PE.n)
            if t >= 2:
                W("vector", PE, x_mm[t - 2])
            dv = I("vector", lambda e, t=t: e.tensor_copy(stair_ph0[t % 2], PB0[:]), DV)
            W("tensor", DV, dv)
            for latc in range(2):
                for dc in range(8):
                    I("tensor", lambda e, latc=latc, dc=dc, t=t: e.matmul(
                        PS_M[:, 128 * latc:128 * (latc + 1)],
                        wdkvT[:, 256 * dc + 128 * latc:256 * dc + 128 * (latc + 1)],
                        stair_ph0[t % 2][:, 128 * dc:128 * (dc + 1)],
                        start=(dc == 0), stop=(dc == 7), skip_group_check=True))
            for dc in range(8):
                I("tensor", lambda e, dc=dc, t=t: e.matmul(
                    PS_M[:, 256:512], stair_ph0[t % 2][:, 128 * dc:128 * (dc + 1)],
                    wdkvT[:, 256 * dc:256 * (dc + 1)],
                    start=(dc == 0), stop=(dc == 7), skip_group_check=True),
                    PE if dc == 7 else None)
            x_mm[t] = PE.n
            W("scalar", PE, PE.n)
            ac2 = None
            for latc in range(2):
                ac2 = I("scalar", lambda e, latc=latc, t=t: e.activation(
                    out=ckvT[latc][:, 128 * t:128 * (t + 1)],
                    in_=PS_M[:, 128 * latc:128 * (latc + 1)], func=AF.Copy), AC)
            W("vector", AC, ac2)
            dv = I("vector", lambda e, t=t: e.tensor_copy(
                ckvkm[:, 256 * t:256 * (t + 1)], PS_M[:, 256:512]), DV)
            W("tensor", AC, ac2)
            W("tensor", DV, dv)

        xq_f2 = xq_f

        qbufs = [out_st[0], out_st[1]]
        red_hist = []

        def plan_quant_load_reduce(i):
            buf = qbufs[i % 2]
            sm = (LQA, LQB)[i % 2]
            if i >= 2:
                W("sync", DV, red_hist[i - 2])
            ld_v = I("sync", lambda e, b=buf, i=i: e.dma_start(out=b[:, 0:C], in_=fk[128 * i:128 * (i + 1), :]), sm, 16)
            W("vector", sm, ld_v)
            v = I("vector", lambda e, b=buf, i=i: e.tensor_reduce(
                smax[:, i:i + 1], b[:, 0:C], AX.X, ALU.max, apply_absolute_value=True), DV)
            red_hist.append(v)

        def plan_quant_scale():
            W("vector", DV, red_hist[-1])
            VE(lambda e: e.tensor_reduce(smax[:, 8:9], smax[:, 0:8], AX.X, ALU.max))
            W("tensor", DV, DV.n)
            v = I("tensor", lambda e: e.transpose(PS_M[0:1, 0:128], smax[:, 8:9], id_f[:]), PE)
            W("vector", PE, v)
            VE(lambda e: e.tensor_reduce(sc1[0:1, 0:1], PS_M[0:1, 0:128], AX.X, ALU.max))
            VE(lambda e: e.tensor_scalar(out=sc1[0:1, 1:2], in0=sc1[0:1, 0:1],
                                         scalar1=1.0 / 7.0, scalar2=None, op0=ALU.mult))
            v = VE(lambda e: e.reciprocal(sc1[0:1, 2:3], sc1[0:1, 1:2]))
            W("tensor", DV, v)
            I("tensor", lambda e: e.matmul(PS_M[:, 0:1], ones_row_f[:], sc1[0:1, 1:2],
                                           start=True, stop=True, skip_group_check=True), PE)
            v = I("tensor", lambda e: e.matmul(PS_M[:, 1:2], ones_row_f[:], sc1[0:1, 2:3],
                                               start=True, stop=True, skip_group_check=True), PE)
            W("scalar", PE, v)
            I("scalar", lambda e: e.activation(out=m7[:], in_=PS_M[:, 0:1], func=AF.Copy), AC)
            v = I("scalar", lambda e: e.activation(out=r7[:], in_=PS_M[:, 1:2], func=AF.Copy), AC)
            W("vector", AC, v)
            W("tensor", AC, v)

        lad_hist = []

        def plan_quant_round(i):
            buf = qbufs[i % 2]
            sm = (LQA, LQB)[i % 2]
            if i >= 2:
                W("sync", DV, lad_hist[i - 2])
            ld_v = I("sync", lambda e, b=buf, i=i: e.dma_start(out=b[:, 0:C], in_=fk[128 * i:128 * (i + 1), :]), sm, 16)
            W("vector", sm, ld_v)
            VE(lambda e, b=buf: e.tensor_scalar(out=x1f_s[:, 0:C], in0=b[:, 0:C],
                                                scalar1=r7[:], scalar2=None, op0=ALU.mult))
            VE(lambda e: e.tensor_copy(qint[:, 0:C], x1f_s[:, 0:C]))
            v = VE(lambda e, i=i: e.tensor_scalar(
                out=wq[i][:], in0=qint[:, 0:C], scalar1=m7[:], scalar2=None, op0=ALU.mult))
            lad_hist.append(v)

        plan_stage(stage_jobs, 4)
        W("tensor", AC, cast_of["wdkv1"])
        for latc in range(2):
            batch8_transpose([wdkv_natb[latc][:, 128 * dc:128 * (dc + 1)] for dc in range(8)],
                             wdkvT.ap().rearrange("p (d t q) -> p d t q", t=2, q=128)[:, :, latc, :])
        for t in range(16):
            x_piece(t)
            plan_stage(stage_jobs, 2)
            if t == 2:
                W("tensor", AC, cast_of["wdq1"])
                for qlc in range(2):
                    batch8_transpose([wdq_nat[qlc][:, 128 * dc:128 * (dc + 1)] for dc in range(8)],
                                     wdqT.ap().rearrange("p (d t q) -> p d t q", t=2, q=128)[:, :, qlc, :])
            elif t == 5:
                W("tensor", AC, cast_of["wuq7"])
                for qlc in range(2):
                    batch8_transpose([wuq_nat8[dc][:, 128 * qlc:128 * (qlc + 1)] for dc in range(8)],
                                     wuqT[qlc].ap().rearrange("p (d q) -> p d q", d=8))
            elif t == 7:
                W("tensor", AC, cast_of["wuk7"])
                for qlc in range(2):
                    for dc in range(8):
                        I("tensor", lambda e, qlc=qlc, dc=dc: e.matmul(
                            PS_M[:, 0:256], wuq_nat8[dc][:, 128 * qlc:128 * (qlc + 1)], wuk_nat[dc],
                            start=(dc == 0), stop=(dc == 7), skip_group_check=True),
                            PE if dc == 7 else None)
                    W("scalar", PE, PE.n)
                    v = I("scalar", lambda e, qlc=qlc: e.activation(out=ke1[qlc], in_=PS_M[:, 0:256], func=AF.Copy), AC)
                    W("tensor", AC, v)
            elif t == 8:
                keff_pe = 0
                for cc in range(8):
                    for qlc in range(2):
                        I("tensor", lambda e, cc=cc, qlc=qlc: e.matmul(
                            PS_M[:, 0:256], wdq_nat[qlc][:, 128 * cc:128 * (cc + 1)], ke1[qlc],
                            start=(qlc == 0), stop=(qlc == 1), skip_group_check=True),
                            PE if qlc == 1 else None)
                    keff_pe = PE.n
                    W("scalar", PE, PE.n)
                    v = I("scalar", lambda e, cc=cc: e.activation(
                        out=keff[:, 256 * cc:256 * (cc + 1)], in_=PS_M[:, 0:256], func=AF.Copy), AC)
                    W("tensor", AC, v)
            elif t == 9:
                plan_stage(wo_jobs, 2, gate=keff_pe)
                plan_quant_load_reduce(0)
                plan_quant_load_reduce(1)
            elif t == 10:
                plan_stage(wo_jobs, 3)
                for i in range(2, 5):
                    plan_quant_load_reduce(i)
            elif t == 11:
                plan_stage(wo_jobs, 3)
                for i in range(5, 8):
                    plan_quant_load_reduce(i)
            elif t == 12:
                plan_quant_scale()
            elif t >= 13:
                plan_quant_round(2 * (t - 13))
                plan_quant_round(2 * (t - 13) + 1)
        plan_stage(stage_jobs, 99)
        for i in range(6, 8):
            plan_quant_round(i)
        W("gpsimd", AC, cast_of[stage_prev[-2]])
        ld_v = I("gpsimd", lambda e: e.dma_start(out=r2[0][0:16, 0:C], in_=fb), SA, 16)
        W("scalar", SA, ld_v)
        I("scalar", lambda e: e.activation(out=b_sb[:], in_=r2[0][0:16, 0:C], func=AF.Copy), AC)
        W("gpsimd", AC, cast_of[stage_prev[-1]])
        ld_v = I("gpsimd", lambda e: e.dma_start(out=r2[1][0:1, 0:C], in_=fbias), SB, 16)
        W("scalar", SB, ld_v)
        I("scalar", lambda e: e.activation(out=bias_sb[:], in_=r2[1][0:1, 0:C], func=AF.Copy), AC)
        W("tensor", AC, cast_of["wo7"])
        woT = []
        for dc in range(8):
            if dc < 4:
                woT.append([qTc(2 * dc), qTc(2 * dc + 1)])
            else:
                woT.append([xqTc(2 * (dc - 4)), xqTc(2 * (dc - 4) + 1)])
        for cc in range(8):
            half, off = divmod(128 * cc, QB)
            W("tensor", DV, DV.n)
            W("tensor", AC, AC.n)
            for dc in range(8):
                I("tensor", lambda e, cc=cc, dc=dc: e.transpose(
                    PB0[:, 128 * dc:128 * (dc + 1)], wo_b[cc][:, 128 * dc:128 * (dc + 1)],
                    id_b[:]), PE if dc == 7 else None)
            W("scalar", PE, PE.n)
            v = None
            for dd in range(8):
                dst = (qT if dd < 4 else xqT)[:, 1024 * (dd % 4) + 512 * half + off:
                                              1024 * (dd % 4) + 512 * half + off + 128]
                v = I("scalar", lambda e, dd=dd, dst=dst: e.activation(
                    out=dst, in_=PB0[:, 128 * dd:128 * (dd + 1)], func=AF.Copy), AC)
            W("tensor", AC, v)
        W("tensor", AC, cast_of["wuv7"])
        for lc in range(2):
            for nb in range(2):
                for dc in range(8):
                    I("tensor", lambda e, lc=lc, nb=nb, dc=dc: e.matmul(
                        PS_M[:], wuv_nat[dc][:, 128 * lc:128 * (lc + 1)], woT[dc][nb],
                        start=(dc == 0), stop=(dc == 7), skip_group_check=True),
                        PE if dc == 7 else None)
                W("scalar", PE, PE.n)
                v = I("scalar", lambda e, lc=lc, nb=nb: e.activation(
                    out=vsb[lc][:, QB * nb:QB * (nb + 1)], in_=PS_M[:], func=AF.Copy), AC)
                W("tensor", AC, v)

        barrier()

        n_out_dma = 0
        last_stair_pe = [0]
        last_stair_dv = [0]
        qprep_pe_hold = [0]
        den_last = [0]
        q0_last = [0]
        for s in range(NSLOT):
            BS = 16 * s + 16
            W("gpsimd", DV, DV.n)
            W("gpsimd", AC, AC.n)
            mk_v = I("gpsimd", lambda e, s=s: e.dma_start(out=masks_sb[:], in_=masks[s]), LD, 16)

            if s == 0:
                W("sync", AC, AC.n)
                W("sync", DV, DV.n)
                pre_cast = []
                for tt in range(4):
                    buf = xq_f[tt % 2]
                    if tt >= 2:
                        W("sync", AC, pre_cast[tt - 2])
                    ld_v = I("sync", lambda e, b=buf, s=s, tt=tt: e.dma_start(
                        out=b[:], in_=xq[QB * s + 128 * tt: QB * s + 128 * (tt + 1), :]), TR, 16)
                    W("scalar", TR, ld_v)
                    W("scalar", PE, PE.n)
                    if tt == 0:
                        W("scalar", DV, DV.n)
                    v = I("scalar", lambda e, b=buf: e.activation(out=x1b[0][:], in_=b[:], func=AF.Copy), AC)
                    pre_cast.append(v)
                    W("tensor", AC, v)
                    W("tensor", DV, DV.n)
                    dst3 = xqT.ap().rearrange("p (d q) -> p d q", d=8)[:, :, 128 * tt:128 * (tt + 1)]
                    transpose_tile_to_xT(x1b[0], dst3)
                W("tensor", DV, DV.n)
                qlow = [y_sb[:, 0:QB], y_sb[:, QB:2 * QB]]
                for qlc in range(2):
                    for dc in range(8):
                        I("tensor", lambda e, qlc=qlc, dc=dc: e.matmul(
                            PS_M[:], wdqT[:, 256 * dc + 128 * qlc:256 * dc + 128 * (qlc + 1)], xqTc(dc),
                            start=(dc == 0), stop=(dc == 7), skip_group_check=True),
                            PE if dc == 7 else None)
                    W("scalar", PE, PE.n)
                    v = I("scalar", lambda e, qlc=qlc: e.activation(out=qlow[qlc], in_=PS_M[:], func=AF.Copy), AC)
                    W("tensor", AC, v)
                for dc in range(8):
                    for qlc in range(2):
                        I("tensor", lambda e, dc=dc, qlc=qlc: e.matmul(
                            PS_M[:], wuqT[qlc][:, 128 * dc:128 * (dc + 1)], qlow[qlc],
                            start=(qlc == 0), stop=(qlc == 1), skip_group_check=True),
                            PE if qlc == 1 else None)
                    W("scalar", PE, PE.n)
                    v = I("scalar", lambda e, dc=dc: e.activation(out=qTc(dc), in_=PS_M[:], func=AF.Copy), AC)
                    W("tensor", AC, v)
                qprep_pe_hold[0] = PE.n

            ac_ctxn, dv_den, ac_qlat = {}, {}, {}
            exp_last, pe_end = {}, {}
            c1_dv, c2_dv, b1_ac, b2_ac, b3_dv, b1_dv, c2_vr = {}, {}, {}, {}, {}, {}, {}
            psm_free = [[(AC, AC.n)]]

            def use_psm():
                for sm, v in psm_free[0]:
                    W("tensor", sm, v)

            def set_psm(sm, v):
                psm_free[0] = [(sm, v)]

            def set_psm_multi(pairs):
                psm_free[0] = list(pairs)

            def prologue_q(h, lc, q0sum=True):
                p = h % 2
                cc, po = h // 2, 64 * (h % 2)
                bank = PS_SUM if (lc == 0 and q0sum) else PS_M
                if lc == 0 and q0sum:
                    W("tensor", DV, den_last[0])
                else:
                    use_psm()
                v = I("tensor", lambda e, cc=cc, po=po, lc=lc, bank=bank: e.matmul(
                    bank[:], keff[po:po + 64, 256 * cc + 128 * lc:256 * cc + 128 * (lc + 1)],
                    qT[po:po + 64, QB * cc:QB * (cc + 1)],
                    start=True, stop=True, skip_group_check=True), PE)
                W("scalar", PE, v)
                v = I("scalar", lambda e, p=p, lc=lc, bank=bank: e.activation(
                    out=qlat[p][lc][:], in_=bank[:], func=AF.Copy), AC)
                if lc == 0 and q0sum:
                    q0_last[0] = v
                else:
                    set_psm(AC, v)
                    if lc == 1:
                        ac_qlat[h] = v

            def prologue_c1(h):
                p = h % 2
                sm = 3 * p
                W("vector", AC, ac_qlat[h])
                VE(lambda e, p=p, sm=sm: e.tensor_reduce(smax[:, sm:sm + 1], qlat[p][0][:], AX.X,
                                                         ALU.max, apply_absolute_value=True))
                VE(lambda e, p=p, sm=sm: e.tensor_reduce(smax[:, sm + 1:sm + 2], qlat[p][1][:], AX.X,
                                                         ALU.max, apply_absolute_value=True))
                c1_dv[h] = VE(lambda e, sm=sm: e.tensor_max(smax[:, sm + 2:sm + 3], smax[:, sm:sm + 1],
                                                            smax[:, sm + 1:sm + 2]))

            def prologue_c2(h):
                p = h % 2
                sm = 3 * p
                sx = 5 + 2 * p
                use_psm()
                W("tensor", DV, c1_dv[h])
                v = I("tensor", lambda e, sm=sm: e.transpose(PS_M[0:1, 0:128], smax[:, sm + 2:sm + 3],
                                                             id_f[:]), PE)
                W("vector", PE, v)
                vr = VE(lambda e, sx=sx: e.tensor_reduce(sc1[0:1, sx:sx + 1], PS_M[0:1, 0:128], AX.X, ALU.max))
                c2_vr[h] = vr
                set_psm(DV, vr)
                c2_dv[h] = VE(lambda e, sx=sx: e.tensor_mul(sc1[0:1, sx + 1:sx + 2], sc1[0:1, sx:sx + 1],
                                                            kfac[0:1, 0:1]))

            def prologue_c3(h):
                p = h % 2
                sx = 5 + 2 * p
                use_psm()
                W("tensor", DV, c2_dv[h])
                v = I("tensor", lambda e, sx=sx: e.matmul(PS_M[:, 0:1], ones_row_f[:],
                                                          sc1[0:1, sx + 1:sx + 2],
                                                          start=True, stop=True, skip_group_check=True), PE)
                W("scalar", PE, v)
                vv = SE(lambda e, h=h: e.activation(out=negc[:, h:h + 1], in_=PS_M[:, 0:1],
                                                    func=AF.Copy, scale=-1.0))
                set_psm(AC, vv)

            def epilogue_a(h, loop_pe):
                hp = h % 2
                W("scalar", PE, loop_pe)
                I("scalar", lambda e, hp=hp: e.activation(out=ctxn[hp][0][:], in_=PS_C[0][:], func=AF.Copy), AC)
                v = I("scalar", lambda e, hp=hp: e.activation(out=ctxn[hp][1][:], in_=PS_C[1][:], func=AF.Copy), AC)
                ac_ctxn[h] = v
                W("vector", PE, loop_pe)
                dv_den[h] = VE(lambda e, hp=hp: e.tensor_copy(den_sb[hp][:], PS_SUM[0:1, :]))
                den_last[0] = dv_den[h]

            def epilogue_b1(h):
                hp = h % 2
                b1_dv[h] = VE(lambda e, hp=hp: e.reciprocal(recip[hp][:], den_sb[hp][:]))

            def epilogue_b1b(h):
                hp = h % 2
                W("scalar", DV, b1_dv[h])
                b1_ac[h] = I("scalar", lambda e, hp=hp: e.activation(out=recip_b[hp][:], in_=recip[hp][:],
                                                                    func=AF.Copy), AC)

            def epilogue_b2(h):
                hp = h % 2
                use_psm()
                W("tensor", AC, b1_ac[h])
                v = I("tensor", lambda e, hp=hp: e.matmul(PS_M[:], ones_row_b[:], recip_b[hp][:],
                                                          start=True, stop=True, skip_group_check=True), PE)
                W("scalar", PE, v)
                b2_ac[h] = I("scalar", lambda e, hp=hp: e.activation(out=rbc[hp][:], in_=PS_M[:], func=AF.Copy), AC)
                set_psm(AC, b2_ac[h])

            def epilogue_b3(h):
                hp = h % 2
                W("vector", AC, b2_ac[h])
                I("vector", lambda e, hp=hp: e.tensor_mul(ctxn[hp][0][:], ctxn[hp][0][:], rbc[hp][:]), DV)
                b3_dv[h] = I("vector", lambda e, hp=hp: e.tensor_mul(ctxn[hp][1][:], ctxn[hp][1][:], rbc[hp][:]), DV)

            def epilogue_c(H):
                use_psm()
                W("tensor", DV, b3_dv[2 * H + 1])
                for hh in range(2):
                    hg = 2 * H + hh
                    for lc in range(2):
                        I("tensor", lambda e, hh=hh, hg=hg, lc=lc: e.matmul(
                            PS_M[64 * hh:64 * hh + 64, :],
                            vsb[lc][:, 64 * hg:64 * hg + 64], ctxn[hh][lc][:],
                            start=(lc == 0), stop=(lc == 1), skip_group_check=True),
                            PE if (hh == 1 and lc == 1) else None)
                W("scalar", PE, PE.n)
                v = I("scalar", lambda e, H=H: e.activation(
                    out=y_sb[:, QB * H:QB * (H + 1)], in_=PS_M[:], func=AF.Copy), AC)
                set_psm(AC, v)
                W("tensor", AC, v)

            slot_start_ac = AC.n
            slot_start_dv = DV.n
            stair_cast, stair_tr, stair_mm_pe, stair_dv = {}, {}, {}, {}

            def stair_dma(h):
                T = 16 * (s + 1) + h
                if h >= 2:
                    W("sync", AC, stair_cast[h - 2])
                else:
                    W("sync", AC, slot_start_ac)
                return I("sync", lambda e, T=T, h=h: e.dma_start(
                    out=xq_f[h % 2][:], in_=xk[128 * T:128 * (T + 1), :]), TR, 16)

            def stair_cast_f(h, ld_v):
                W("scalar", TR, ld_v)
                if h >= 2:
                    W("scalar", TR, stair_tr[h - 2])
                else:
                    W("scalar", DV, slot_start_dv)
                stair_cast[h] = I("scalar", lambda e, h=h: e.activation(
                    out=x1b[h % 2][:], in_=xq_f[h % 2][:], func=AF.Copy), AC)

            def stair_trans(h):
                W("sync", AC, stair_cast[h])
                if h >= 1:
                    W("sync", PE, stair_mm_pe[h - 1])
                else:
                    W("sync", PE, last_stair_pe[0])
                for dc in range(8):
                    I("sync", lambda e, dc=dc, h=h: e.dma_start(
                        out=stairT[:, 128 * dc:128 * (dc + 1)],
                        in_=x1b[h % 2][:, 128 * dc:128 * (dc + 1)], transpose=True), TR, 16)
                stair_tr[h] = TR.n

            def stair_mm(h):
                T = 16 * (s + 1) + h
                W("tensor", TR, stair_tr[h])
                W("tensor", DV, stair_dv.get(h - 1, last_stair_dv[0]))
                for dc in range(8):
                    I("tensor", lambda e, dc=dc: e.matmul(
                        PS_M[:, 256:512], stairT[:, 128 * dc:128 * (dc + 1)],
                        wdkvT[:, 256 * dc:256 * (dc + 1)],
                        start=(dc == 0), stop=(dc == 7), skip_group_check=True))
                W("tensor", DV, c2_vr.get(h + 1, c2_vr.get(h, 0)))
                for latc in range(2):
                    for dc in range(8):
                        I("tensor", lambda e, latc=latc, dc=dc: e.matmul(
                            PS_M[:, 128 * latc:128 * (latc + 1)],
                            wdkvT[:, 256 * dc + 128 * latc:256 * dc + 128 * (latc + 1)],
                            stairT[:, 128 * dc:128 * (dc + 1)],
                            start=(dc == 0), stop=(dc == 7), skip_group_check=True),
                            PE if (latc == 1 and dc == 7) else None)
                stair_mm_pe[h] = PE.n
                last_stair_pe[0] = PE.n
                W("scalar", PE, PE.n)
                ac2 = None
                for latc in range(2):
                    ac2 = I("scalar", lambda e, latc=latc, T=T: e.activation(
                        out=ckvT[latc][:, 128 * T:128 * (T + 1)],
                        in_=PS_M[:, 128 * latc:128 * (latc + 1)], func=AF.Copy), AC)
                W("vector", AC, ac2)
                dv = I("vector", lambda e, T=T: e.tensor_copy(
                    ckvkm[:, 256 * T:256 * (T + 1)], PS_M[:, 256:512]), DV)
                stair_dv[h] = dv
                last_stair_dv[0] = dv
                set_psm(DV, dv)

            c0, c1 = 2048 * s, 2048 * (s + 1)
            VE(lambda e, c0=c0, c1=c1: e.tensor_reduce(smax[:, 0:1], ckvT[0][:, c0:c1], AX.X,
                                                       ALU.max, apply_absolute_value=True))
            VE(lambda e, c0=c0, c1=c1: e.tensor_reduce(smax[:, 1:2], ckvT[1][:, c0:c1], AX.X,
                                                       ALU.max, apply_absolute_value=True))
            v = VE(lambda e: e.tensor_max(smax[:, 2:3], smax[:, 0:1], smax[:, 1:2]))
            W("tensor", DV, v)
            use_psm()
            v = I("tensor", lambda e: e.transpose(PS_M[0:1, 0:128], smax[:, 2:3], id_f[:]), PE)
            W("vector", PE, v)
            VE(lambda e: e.tensor_reduce(sc1[0:1, 9:10], PS_M[0:1, 0:128], AX.X, ALU.max))
            VE(lambda e: e.tensor_max(sc1[0:1, 3:4], sc1[0:1, 3:4], sc1[0:1, 9:10]))
            v = VE(lambda e: e.tensor_scalar(out=sc1[0:1, 4:5], in0=sc1[0:1, 3:4],
                                             scalar1=32.0, scalar2=None, op0=ALU.mult))
            W("tensor", DV, v)
            v = I("tensor", lambda e: e.matmul(PS_M[:, 0:1], ones_row_f[:], sc1[0:1, 4:5],
                                               start=True, stop=True, skip_group_check=True), PE)
            W("scalar", PE, v)
            kf_v = SE(lambda e: e.activation(out=kfac[:], in_=PS_M[:, 0:1], func=AF.Copy))
            set_psm(AC, kf_v)

            st_ld_next = [0]
            if s < 3:
                st_ld_next[0] = stair_dma(0)
            prologue_q(0, 0, q0sum=False)
            prologue_q(0, 1)
            prologue_c1(0)
            prologue_c2(0)
            prologue_c3(0)

            for h in range(NH):
                hp = h % 2
                p = h % 2
                exp_ac, msk_dv, s_pe, pacc_dv, sum_pe, ctx_pe = {}, {}, {}, {}, {}, {}

                def plan_S(kc, h=h, p=p, exp_ac=exp_ac, s_pe=s_pe):
                    par = kc % 4
                    if kc >= 4:
                        W("tensor", AC, exp_ac[kc - 4])
                    elif h > 0:
                        W("tensor", AC, exp_last[h - 1][kc % 4])
                    if kc == 0:
                        W("tensor", AC, ac_qlat.get(h, 0))
                    for lc in range(2):
                        I("tensor", lambda e, kc=kc, lc=lc, par=par, p=p: e.matmul(
                            PS_S[par], ckvT[lc][:, 128 * kc:128 * (kc + 1)], qlat[p][lc][:],
                            start=(lc == 0), stop=(lc == 1), skip_group_check=True),
                            PE if lc == 1 else None)
                    s_pe[kc] = PE.n

                def plan_exp(kc, h=h, s=s, exp_ac=exp_ac, msk_dv=msk_dv,
                             s_pe=s_pe, pacc_dv=pacc_dv, ctx_pe=ctx_pe):
                    par = kc % 4
                    if kc >= 4:
                        W("scalar", PE, ctx_pe[kc - 4])
                        W("scalar", DV, pacc_dv[kc - 4])
                    else:
                        W("scalar", PE, s_pe[kc])
                        if h > 0:
                            W("scalar", DV, dv_den[h - 1])
                    v = I("scalar", lambda e, par=par, h=h: e.activation(
                        out=PT[par][:], in_=PS_S[par], func=AF.Exp, scale=0.125,
                        bias=negc[:, h:h + 1]), AC)
                    exp_ac[kc] = v
                    W("vector", AC, v)
                    if kc >= 16 * s:
                        w_i = kc - 16 * s
                        msk_dv[kc] = I("vector", lambda e, par=par, w_i=w_i: e.tensor_mul(
                            PT[par][:], PT[par][:], masks_sb[:, 512 * w_i:512 * (w_i + 1)]), DV)
                    g = kc % 8
                    pb = (kc // 8) % 2
                    if g == 0:
                        gi = kc // 8
                        if gi >= 2:
                            W("vector", PE, sum_pe[gi - 2])
                        elif h > 0:
                            W("vector", PE, pe_end[h - 1])
                        pacc_dv[kc] = I("vector", lambda e, par=par, pb=pb: e.tensor_copy(
                            Pacc[pb][:], PT[par][:]), DV)
                    else:
                        pacc_dv[kc] = I("vector", lambda e, par=par, pb=pb: e.tensor_add(
                            Pacc[pb][:], Pacc[pb][:], PT[par][:]), DV)

                def plan_ctx(kc, h=h, BS=BS, exp_ac=exp_ac, msk_dv=msk_dv,
                             pacc_dv=pacc_dv, sum_pe=sum_pe, ctx_pe=ctx_pe):
                    par = kc % 4
                    if kc in msk_dv:
                        W("tensor", DV, msk_dv[kc])
                    else:
                        W("tensor", AC, exp_ac[kc])
                    first, last = (kc == 0), (kc == BS - 1)
                    if first and h > 0:
                        W("tensor", AC, ac_ctxn[h - 1])
                        W("tensor", DV, dv_den[h - 1])
                    for lc in range(2):
                        I("tensor", lambda e, kc=kc, lc=lc, par=par: e.matmul(
                            PS_C[lc][:], ckvkm[:, 256 * kc + 128 * lc:256 * kc + 128 * (lc + 1)],
                            PT[par][:], start=first, stop=last, skip_group_check=True),
                            PE if lc == 1 else None)
                    ctx_pe[kc] = PE.n
                    if kc % 8 == 7:
                        pb = (kc // 8) % 2
                        W("tensor", DV, pacc_dv[kc])
                        if kc == 7:
                            W("tensor", AC, q0_last[0])
                        I("tensor", lambda e, pb=pb, kc=kc, BS=BS: e.matmul(
                            PS_SUM[0:1, :], ones_col[:], Pacc[pb][:],
                            start=(kc == 7), stop=(kc == BS - 1), skip_group_check=True), PE)
                        sum_pe[kc // 8] = PE.n

                if h == 0:
                    W("scalar", LD, mk_v)
                for k0 in range(min(4, BS)):
                    plan_S(k0)
                    plan_exp(k0)
                for kc in range(BS):
                    if kc + 4 < BS:
                        plan_S(kc + 4)
                    plan_ctx(kc)
                    if kc + 4 < BS:
                        plan_exp(kc + 4)
                    if kc == 1 and h > 0:
                        epilogue_b1(h - 1)
                    elif kc == 2 and s < 3:
                        stair_cast_f(h, st_ld_next[0])
                    elif kc == 6 and s < 3 and h + 1 < NH:
                        st_ld_next[0] = stair_dma(h + 1)
                    elif kc == 3 and h + 1 < NH:
                        prologue_q(h + 1, 0, q0sum=False)
                    elif kc == 4:
                        if s < 3:
                            stair_trans(h)
                        if h > 0:
                            epilogue_b1b(h - 1)
                    elif kc == 5 and h + 1 < NH:
                        prologue_q(h + 1, 1)
                        prologue_c1(h + 1)
                    elif kc == 9 and h + 1 < NH:
                        prologue_c2(h + 1)
                    elif kc == 8 and h > 0:
                        epilogue_b2(h - 1)
                    elif kc == 10 and h > 0:
                        epilogue_b3(h - 1)
                    elif kc == 15 and h >= 2 and h % 2 == 0:
                        epilogue_c((h - 2) // 2)
                    elif kc == 13 and s < 3:
                        stair_mm(h)
                    elif kc == 14 and h + 1 < NH:
                        prologue_c3(h + 1)
                exp_last[h] = {kc % 4: exp_ac[kc] for kc in (BS - 4, BS - 3, BS - 2, BS - 1)}
                pe_end[h] = PE.n
                epilogue_a(h, PE.n)
            epilogue_b1(NH - 1)
            epilogue_b1b(NH - 1)
            epilogue_b2(NH - 1)
            epilogue_b3(NH - 1)
            epilogue_c(NH // 2 - 1)
            ysb_ac = AC.n

            def plan_ffn(tt):
                par = tt % 2
                if tt >= 1:
                    W("tensor", DV, r2_dv[tt - 1])
                else:
                    W("tensor", AC, ac_ctxn[NH - 1])
                    W("tensor", DV, dv_den[NH - 1])
                W("tensor", AC, uT_ac[tt])
                for fbk in range(2):
                    bank = PS_C[fbk]
                    for dc in range(8):
                        I("tensor", lambda e, dc=dc, fbk=fbk, bank=bank: e.matmul(
                            bank[:], x1T[:, 128 * dc:128 * (dc + 1)], wq[dc][:, QB * fbk:QB * (fbk + 1)],
                            start=(dc == 0), stop=False, skip_group_check=True))
                    I("tensor", lambda e, fbk=fbk, bank=bank: e.matmul(
                        bank[:], uT[:], b_sb[:, QB * fbk:QB * (fbk + 1)],
                        start=False, stop=False, skip_group_check=True))
                    I("tensor", lambda e, fbk=fbk, bank=bank: e.matmul(
                        bank[:], ones_row_b[:], bias_sb[0:1, QB * fbk:QB * (fbk + 1)],
                        start=False, stop=True, skip_group_check=True), PE)
                    W("vector", PE, PE.n)
                    I("vector", lambda e, fbk=fbk, par=par, bank=bank: e.tensor_add(
                        r2[par][:, QB * fbk:QB * (fbk + 1)], x1b[par][:, QB * fbk:QB * (fbk + 1)],
                        bank[:]), DV)
                return DV.n

            def plan_ln(src, scratch, b0):
                sq = I("scalar", lambda e, src=src, scratch=scratch: e.activation(
                    out=scratch, in_=src, func=AF.Square, accum_out=stats[:, b0:b0 + 1]), AC)
                VE(lambda e, src=src: e.tensor_reduce(stats[:, b0 + 1:b0 + 2], src, AX.X, ALU.add))
                W("vector", AC, sq)
                VE(lambda e: e.tensor_scalar(out=stats[:, b0 + 4:b0 + 5], in0=stats[:, b0 + 1:b0 + 2],
                                             scalar1=stats[:, b0 + 1:b0 + 2], scalar2=1.0 / (1024.0 * 1024.0),
                                             op0=ALU.mult, op1=ALU.mult))
                v = VE(lambda e: e.scalar_tensor_tensor(
                    out=stats[:, b0 + 5:b0 + 6], in0=stats[:, b0:b0 + 1], scalar=1.0 / 1024,
                    in1=stats[:, b0 + 4:b0 + 5], op0=ALU.mult, op1=ALU.subtract))
                W("scalar", DV, v)
                SE(lambda e: e.activation(out=stats[:, b0 + 6:b0 + 7], in_=stats[:, b0 + 5:b0 + 6],
                                          func=AF.Ln, bias=eps_t[:]))
                v = SE(lambda e: e.activation(out=stats[:, b0 + 7:b0 + 8], in_=stats[:, b0 + 6:b0 + 7],
                                              func=AF.Exp, scale=-0.5))
                W("vector", AC, v)
                v = VE(lambda e: e.scalar_tensor_tensor(
                    out=stats[:, b0 + 9:b0 + 10], in0=stats[:, b0 + 1:b0 + 2], scalar=-1.0 / 1024,
                    in1=stats[:, b0 + 7:b0 + 8], op0=ALU.mult, op1=ALU.mult))
                W("scalar", DV, v)

            ln1_ac, x1T_dv, r2_dv, uT_ac = {}, {}, {}, {}
            r1_dv = {}
            prep_cast2, prep_tr2, pld_of = {}, {}, {}
            pa_dv0 = DV.n
            for tt in range(5):
                par = tt % 2
                if tt >= 1:
                    pv = tt - 1
                    parv = pv % 2
                    W("tensor", AC, ln1_ac[pv])
                    if pv >= 1:
                        W("tensor", DV, r1_dv[pv])
                    for dc in range(8):
                        I("tensor", lambda e, dc=dc, parv=parv: e.transpose(
                            PB0[:, 128 * dc:128 * (dc + 1)], x1b[parv][:, 128 * dc:128 * (dc + 1)],
                            id_b[:]), PE if dc == 7 else None)
                    W("vector", PE, PE.n)
                    x1T_dv[pv] = I("vector", lambda e: e.tensor_copy(x1T[:], PB0[:]), DV)
                if tt < 4:
                    if tt >= 2:
                        W("sync", AC, ln1_ac[tt - 2])
                        W("sync", DV, x1T_dv.get(tt - 2, 0))
                    elif s < 3:
                        W("sync", AC, stair_cast[14 + tt])
                    ld_v = I("sync", lambda e, s=s, tt=tt, par=par: e.dma_start(
                        out=xq_f[par][:], in_=xq[QB * s + 128 * tt: QB * s + 128 * (tt + 1), :]), TR, 16)
                    if s + 1 < NSLOT:
                        def stg_of(k):
                            return masks_sb.ap().bitcast(F32)[:, 1024 * (k % 2):1024 * (k % 2 + 1)]

                        def pld_issue(k):
                            if k < 2:
                                W("sync", DV, pa_dv0)
                            else:
                                W("sync", AC, prep_cast2[k - 2])
                            pld_of[k] = I("sync", lambda e, stg=stg_of(k), s=s, k=k: e.dma_start(
                                out=stg, in_=xq[QB * (s + 1) + 128 * k: QB * (s + 1) + 128 * (k + 1), :]),
                                (TGA, TGB)[k % 2], 16)

                        if tt == 0:
                            pld_issue(0)
                            pld_issue(1)
                        W("scalar", (TGA, TGB)[tt % 2], pld_of[tt])
                        if tt == 0:
                            W("scalar", PE, last_stair_pe[0])
                        else:
                            W("scalar", TR, prep_tr2[tt - 1])
                        prep_cast2[tt] = I("scalar", lambda e, stg=stg_of(tt): e.activation(
                            out=stairT[:], in_=stg, func=AF.Copy), AC)
                        if tt + 2 < 4:
                            pld_issue(tt + 2)
                        W("sync", AC, prep_cast2[tt])
                        if tt == 0:
                            W("sync", PE, qprep_pe_hold[0])
                        for dc in range(8):
                            I("sync", lambda e, dc=dc, tt=tt: e.dma_start(
                                out=xqT[:, QB * dc + 128 * tt:QB * dc + 128 * (tt + 1)],
                                in_=stairT[:, 128 * dc:128 * (dc + 1)], transpose=True), TR, 16)
                        prep_tr2[tt] = TR.n
                    if tt == 0:
                        W("tensor", AC, ysb_ac)
                    else:
                        W("tensor", DV, x1T_dv[tt - 1])
                    for Hc in range(8):
                        I("tensor", lambda e, Hc=Hc, tt=tt: e.transpose(
                            PB0[:, 128 * Hc:128 * (Hc + 1)],
                            y_sb[:, QB * Hc + 128 * tt: QB * Hc + 128 * (tt + 1)], id_b[:]),
                            PE if Hc == 7 else None)
                    ytr_pe = PE.n
                if tt >= 1:
                    pv = tt - 1
                    use_psm()
                    W("tensor", DV, x1T_dv[pv])
                    for dc in range(8):
                        I("tensor", lambda e, dc=dc: e.matmul(
                            PS_M[0:16, 0:128], a_sb[:, 16 * dc:16 * (dc + 1)],
                            x1T[:, 128 * dc:128 * (dc + 1)],
                            start=(dc == 0), stop=(dc == 7), skip_group_check=True),
                            PE if dc == 7 else None)
                    W("scalar", PE, PE.n)
                    uT_ac[pv] = I("scalar", lambda e: e.activation(
                        out=uT[:], in_=PS_M[0:16, 0:128], func=AF.Copy), AC)
                    set_psm(AC, uT_ac[pv])
                if tt < 4:
                    W("vector", PE, ytr_pe)
                    W("vector", TR, ld_v)
                    r1_dv[tt] = I("vector", lambda e, par=par: e.tensor_add(
                        xq_f[par][:], xq_f[par][:], PB0[:]), DV)
                    W("scalar", DV, r1_dv[tt])
                    if tt >= 2:
                        W("scalar", DV, r2_dv[tt - 2])
                    elif s < 3:
                        W("scalar", TR, stair_tr[14 + tt])
                    plan_ln(xq_f[par][:], r2[par][:], 0)
                    v = I("scalar", lambda e, par=par: e.activation(
                        out=x1b[par][:], in_=xq_f[par][:], func=AF.Identity,
                        scale=stats[:, 7:8], bias=stats[:, 9:10]), AC)
                    ln1_ac[tt] = v
                if tt >= 1:
                    r2_dv[pv] = plan_ffn(pv)
                    W("scalar", DV, r2_dv[pv])
                    if n_out_dma >= 2:
                        W("scalar", OT, 16 * (n_out_dma - 1))
                    plan_ln(r2[parv][:], out_st[parv][:], 12)
                    ov = I("scalar", lambda e, parv=parv: e.activation(
                        out=out_st[parv][:], in_=r2[parv][:], func=AF.Identity,
                        scale=stats[:, 19:20], bias=stats[:, 21:22]), AC)
                    W("gpsimd", AC, ov)
                    I("gpsimd", lambda e, s=s, pv=pv, parv=parv: e.dma_start(
                        out=out_hbm[QB * s + 128 * pv: QB * s + 128 * (pv + 1), :],
                        in_=out_st[parv][:]), OT, 16)
                    n_out_dma += 1
            if s + 1 < NSLOT:
                W("tensor", TR, prep_tr2[3])
                qlow2 = [Pacc[0][:], Pacc[1][:]]
                for qlc in range(2):
                    use_psm()
                    for dc in range(8):
                        I("tensor", lambda e, qlc=qlc, dc=dc: e.matmul(
                            PS_M[:], wdqT[:, 256 * dc + 128 * qlc:256 * dc + 128 * (qlc + 1)], xqTc(dc),
                            start=(dc == 0), stop=(dc == 7), skip_group_check=True),
                            PE if dc == 7 else None)
                    W("scalar", PE, PE.n)
                    v = I("scalar", lambda e, qlc=qlc: e.activation(out=qlow2[qlc], in_=PS_M[:], func=AF.Copy), AC)
                    set_psm(AC, v)
                for dc in range(8):
                    use_psm()
                    for qlc in range(2):
                        I("tensor", lambda e, dc=dc, qlc=qlc: e.matmul(
                            PS_M[:], wuqT[qlc][:, 128 * dc:128 * (dc + 1)], qlow2[qlc],
                            start=(qlc == 0), stop=(qlc == 1), skip_group_check=True),
                            PE if qlc == 1 else None)
                    W("scalar", PE, PE.n)
                    v = I("scalar", lambda e, dc=dc: e.activation(out=qTc(dc), in_=PS_M[:], func=AF.Copy), AC)
                    set_psm(AC, v)
                qprep_pe_hold[0] = PE.n

        W("gpsimd", OT, OT.n)

        blk = E(nc.Block())

        @blk.tensor
        def _(eng):
            for fn in plan["tensor"]:
                fn(eng)

        @blk.scalar
        def _(eng):
            for fn in plan["scalar"]:
                fn(eng)

        @blk.vector
        def _(eng):
            for fn in plan["vector"]:
                fn(eng)

        @blk.sync
        def _(eng):
            for fn in plan["sync"]:
                fn(eng)

        @blk.gpsimd
        def _(eng):
            for fn in plan["gpsimd"]:
                fn(eng)

    return nc


def _host_masks(c):
    m = np.zeros((NSLOT, 128, 16 * 512), dtype=np.float32)
    tri = (np.arange(128)[:, None] <= np.arange(128)[None, :]).astype(np.float32)
    for s in range(NSLOT):
        for w in range(16):
            for j in range(4):
                mt = 4 * c + j
                if w < mt:
                    m[s, :, 512 * w + 128 * j: 512 * w + 128 * (j + 1)] = 1.0
                elif w == mt:
                    m[s, :, 512 * w + 128 * j: 512 * w + 128 * (j + 1)] = tri
    return m.astype(BF)


def _gather_rows(c):
    idx = []
    for s in range(NSLOT):
        for j in range(4):
            mt = 16 * s + 4 * c + j
            idx.append(np.arange(128 * mt, 128 * (mt + 1)))
    return np.concatenate(idx)


def kernel(L0_attn_x, L0_attn_W_dq, L0_attn_W_uq, L0_attn_W_dkv, L0_attn_W_uk,
           L0_attn_W_uv, L0_attn_W_o, L0_ffn_kernel, L0_ffn_bias, L0_ffn_a, L0_ffn_b):
    x = np.ascontiguousarray(np.asarray(L0_attn_x, dtype=np.float32))
    common = {
        "w_dq": np.ascontiguousarray(np.asarray(L0_attn_W_dq, np.float32)),
        "w_uq": np.ascontiguousarray(np.asarray(L0_attn_W_uq, np.float32)),
        "w_dkv": np.ascontiguousarray(np.asarray(L0_attn_W_dkv, np.float32)),
        "w_uk": np.ascontiguousarray(np.asarray(L0_attn_W_uk, np.float32)),
        "w_uv": np.ascontiguousarray(np.asarray(L0_attn_W_uv, np.float32)),
        "w_o": np.ascontiguousarray(np.asarray(L0_attn_W_o, np.float32)),
        "fk": np.ascontiguousarray(np.asarray(L0_ffn_kernel, np.float32)),
        "fbias": np.ascontiguousarray(np.asarray(L0_ffn_bias, np.float32).reshape(1, C)),
        "fa": np.ascontiguousarray(np.asarray(L0_ffn_a, np.float32)),
        "fb": np.ascontiguousarray(np.asarray(L0_ffn_b, np.float32)),
        "ident_b": np.eye(128, dtype=BF),
        "ident_f": np.eye(128, dtype=np.float32),
    }
    in_maps = []
    gathers = []
    for j in range(8):
        b, c = j // 4, j % 4
        rows = _gather_rows(c)
        gathers.append((b, rows))
        in_maps.append({
            **common,
            "xk": np.ascontiguousarray(x[b]),
            "xq": np.ascontiguousarray(x[b][rows]),
            "masks": _host_masks(c),
        })
    nc = build_nc()
    res = run_bass_kernel_spmd(nc, in_maps, core_ids=list(range(8)))
    out = np.empty((B, T, C), dtype=np.float32)
    for j in range(8):
        b, rows = gathers[j]
        out[b][rows] = np.asarray(res.results[j]["out"], dtype=np.float32)
    return out


# version 47
# speedup vs baseline: 1.0229x; 1.0014x over previous
import numpy as np
import ml_dtypes
from contextlib import ExitStack

import concourse.bass as bass
import concourse.mybir as mybir
from concourse.bass_utils import run_bass_kernel_spmd

F32 = mybir.dt.float32
BF16 = mybir.dt.bfloat16
AF = mybir.ActivationFunctionType
ALU = mybir.AluOpType
AX = mybir.AxisListType

B, T, C = 2, 8192, 1024
NH, HS, KVL, QL = 16, 64, 256, 256
NSLOT, QB = 4, 512
NKG = T // 512
NKC = T // 128
EPS = 1e-5
BF = ml_dtypes.bfloat16
I32 = mybir.dt.int32


def build_nc():
    nc = bass.Bass("TRN2", target_bir_lowering=False)

    xk = nc.dram_tensor("xk", [T, C], F32, kind="ExternalInput").ap()
    xq = nc.dram_tensor("xq", [NSLOT * QB, C], F32, kind="ExternalInput").ap()
    masks = nc.dram_tensor("masks", [NSLOT, 128, 16 * 512], BF16, kind="ExternalInput").ap()
    w_dq = nc.dram_tensor("w_dq", [QL, C], F32, kind="ExternalInput").ap()
    w_uq = nc.dram_tensor("w_uq", [C, QL], F32, kind="ExternalInput").ap()
    w_dkv = nc.dram_tensor("w_dkv", [KVL, C], F32, kind="ExternalInput").ap()
    w_uk = nc.dram_tensor("w_uk", [C, KVL], F32, kind="ExternalInput").ap()
    w_uv = nc.dram_tensor("w_uv", [C, KVL], F32, kind="ExternalInput").ap()
    w_o = nc.dram_tensor("w_o", [C, C], F32, kind="ExternalInput").ap()
    fk = nc.dram_tensor("fk", [C, C], F32, kind="ExternalInput").ap()
    fbias = nc.dram_tensor("fbias", [1, C], F32, kind="ExternalInput").ap()
    fa = nc.dram_tensor("fa", [C, 16], F32, kind="ExternalInput").ap()
    fb = nc.dram_tensor("fb", [16, C], F32, kind="ExternalInput").ap()
    ident_b = nc.dram_tensor("ident_b", [128, 128], BF16, kind="ExternalInput").ap()
    ident_f = nc.dram_tensor("ident_f", [128, 128], F32, kind="ExternalInput").ap()
    out_hbm = nc.dram_tensor("out", [NSLOT * QB, C], F32, kind="ExternalOutput").ap()

    with ExitStack() as stack:
        E = stack.enter_context

        ckvT = [E(nc.sbuf_tensor(f"ckvT{i}", [128, T], BF16)) for i in range(2)]
        ckvkm = E(nc.sbuf_tensor("ckvkm", [128, NKC * 256], BF16))
        masks_sb = E(nc.sbuf_tensor("masks_sb", [128, 16 * 512], BF16))
        xq_f = [E(nc.sbuf_tensor(f"xq_f{i}", [128, C], F32)) for i in range(2)]
        xqT = E(nc.sbuf_tensor("xqT", [128, 8 * QB], BF16))
        qT = E(nc.sbuf_tensor("qT", [128, 8 * QB], BF16))
        qlat = [[E(nc.sbuf_tensor(f"qlat{p}{i}", [128, QB], BF16)) for i in range(2)]
                for p in range(2)]
        PT = [E(nc.sbuf_tensor(f"PT{i}", [128, QB], BF16)) for i in range(4)]
        Pacc = [E(nc.sbuf_tensor(f"Pacc{i}", [128, QB], BF16)) for i in range(2)]
        ctxn = [[E(nc.sbuf_tensor(f"ctxn{i}{l}", [128, QB], BF16)) for l in range(2)]
                for i in range(2)]
        y_sb = E(nc.sbuf_tensor("y_sb", [128, 8 * QB], BF16))
        x1b = [E(nc.sbuf_tensor(f"x1b{i}", [128, C], BF16)) for i in range(2)]
        x1T = E(nc.sbuf_tensor("x1T", [128, C], BF16))
        r2 = [E(nc.sbuf_tensor(f"r2_{i}", [128, C], F32)) for i in range(2)]
        out_st = [E(nc.sbuf_tensor(f"out_st{i}", [128, C], F32)) for i in range(2)]
        wdqT = E(nc.sbuf_tensor("wdqT", [128, 8 * QL], BF16))
        wuqT = [E(nc.sbuf_tensor(f"wuqT{i}", [128, C], BF16)) for i in range(2)]
        wdkvT = E(nc.sbuf_tensor("wdkvT", [128, 8 * KVL], BF16))
        keff = E(nc.sbuf_tensor("keff", [128, 8 * KVL], BF16))
        vsb = [E(nc.sbuf_tensor(f"vsb{i}", [128, C], BF16)) for i in range(2)]
        wq = [E(nc.sbuf_tensor(f"wq{i}", [128, C], BF16)) for i in range(8)]
        a_sb = E(nc.sbuf_tensor("a_sb", [128, 8 * 16], BF16))
        b_sb = E(nc.sbuf_tensor("b_sb", [16, C], BF16))
        bias_sb = E(nc.sbuf_tensor("bias_sb", [1, C], BF16))
        id_b = E(nc.sbuf_tensor("id_b", [128, 128], BF16))
        id_f = E(nc.sbuf_tensor("id_f", [128, 128], F32))
        ones_col = E(nc.sbuf_tensor("ones_col", [128, 1], BF16))
        ones_row_b = E(nc.sbuf_tensor("ones_row_b", [1, 128], BF16))
        ones_row_f = E(nc.sbuf_tensor("ones_row_f", [1, 128], F32))
        uT = E(nc.sbuf_tensor("uT", [16, 128], BF16))
        stats = E(nc.sbuf_tensor("stats", [128, 24], F32))
        smax = E(nc.sbuf_tensor("smax", [128, 16], F32))
        negc = E(nc.sbuf_tensor("negc", [128, 16], F32))
        kfac = E(nc.sbuf_tensor("kfac", [128, 1], F32))
        eps_t = E(nc.sbuf_tensor("eps_t", [128, 1], F32))
        m7 = E(nc.sbuf_tensor("m7", [128, 1], F32))
        r7 = E(nc.sbuf_tensor("r7", [128, 1], F32))
        sc1 = E(nc.sbuf_tensor("sc1", [1, 12], F32))
        recip = [E(nc.sbuf_tensor(f"recip{p}", [1, QB], F32)) for p in range(2)]
        recip_b = [E(nc.sbuf_tensor(f"recip_b{p}", [1, QB], BF16)) for p in range(2)]
        rbc = [E(nc.sbuf_tensor(f"rbc{p}", [128, QB], F32)) for p in range(2)]
        den_sb = [E(nc.sbuf_tensor(f"den{p}", [1, QB], F32)) for p in range(2)]
        stairT = E(nc.sbuf_tensor("stairT", [128, C], BF16))

        def xqTc(dc):
            return xqT[:, QB * dc:QB * (dc + 1)]

        def qTc(dc):
            return qT[:, QB * dc:QB * (dc + 1)]

        PS_S3 = [E(nc.psum_tensor(f"ps_s{i}", [128, QB], F32)) for i in range(3)]
        PS_C = [E(nc.psum_tensor(f"ps_c{i}", [128, QB], F32)) for i in range(2)]
        PS_SUM = E(nc.psum_tensor("ps_sum", [128, QB], F32))
        PS_M = E(nc.psum_tensor("ps_m", [128, QB], F32))
        PB0 = E(nc.psum_tensor("pb0", [128, 1024], BF16))
        PS_S = [t[:] for t in PS_S3] + [PB0.ap().bitcast(F32)]

        class Sem:
            def __init__(self, name):
                self.h = E(nc.semaphore(name))
                self.n = 0

        LD = Sem("LD")
        SA = Sem("SA")
        SB = Sem("SB")
        LQA = Sem("LQA")
        LQB = Sem("LQB")
        TGA = Sem("TGA")
        TGB = Sem("TGB")
        TR = Sem("TR")
        PE = Sem("PE")
        AC = Sem("AC")
        DV = Sem("DV")
        OT = Sem("OT")

        plan = {e: [] for e in ("tensor", "scalar", "vector", "sync", "gpsimd")}

        def P(e, fn):
            plan[e].append(fn)

        def W(e, sem, val):
            if val > 0:
                P(e, lambda eng, sh=sem.h, v=val: eng.wait_ge(sh, v))

        def barrier():
            vals = [(sm, sm.n) for sm in (LD, SA, SB, LQA, LQB, TGA, TGB, TR, PE, AC, DV, OT)]
            for e in plan:
                for sm, v in vals:
                    W(e, sm, v)

        def I(e, fn, sem=None, k=1):
            if sem is None:
                P(e, lambda eng, f=fn: f(eng))
                return None
            sem.n += k
            P(e, lambda eng, f=fn, sh=sem.h, kk=k: f(eng).then_inc(sh, kk))
            return sem.n

        def VE(fn):
            v = I("vector", fn, DV)
            W("vector", DV, v)
            return v

        def SE(fn):
            v = I("scalar", fn, AC)
            W("scalar", AC, v)
            return v

        def transpose_tile_to_xT(src_bf, dst_view3):
            for dc in range(8):
                I("tensor", lambda e, dc=dc: e.transpose(
                    PB0[:, 128 * dc:128 * (dc + 1)], src_bf[:, 128 * dc:128 * (dc + 1)], id_b[:]),
                    PE if dc == 7 else None)
            pe_v = PE.n
            W("vector", PE, pe_v)
            dv_v = I("vector", lambda e, d=dst_view3: e.tensor_copy(
                d, PB0.ap().rearrange("p (d q) -> p d q", d=8)), DV)
            return pe_v, dv_v

        wdq_nat = [masks_sb[:, i * C:(i + 1) * C] for i in range(2)]
        wuq_nat8 = [masks_sb[:, 2 * C + i * 256: 2 * C + (i + 1) * 256] for i in range(8)]
        wuk_nat = [masks_sb[:, 4 * C + i * 256: 4 * C + (i + 1) * 256] for i in range(8)]
        wuv_nat = [ckvkm[:, 8 * C + i * 256: 8 * C + (i + 1) * 256] for i in range(8)]
        wdkv_natb = [y_sb[:, 0:C], y_sb[:, C:2 * C]]
        ke1 = [y_sb[:, 2 * C + i * 256: 2 * C + (i + 1) * 256] for i in range(2)]
        stair_ph0 = [ckvkm[:, 12 * C:13 * C], ckvkm[:, 13 * C:14 * C]]
        qint = ckvkm[:, 14 * C:16 * C].bitcast(I32)
        x1f_s = ckvkm[:, 4 * C:6 * C].bitcast(F32)

        I("gpsimd", lambda e: e.dma_start(out=id_b[:], in_=ident_b), LD, 16)
        I("gpsimd", lambda e: e.dma_start(out=id_f[:], in_=ident_f), LD, 16)
        I("vector", lambda e: e.memset(ones_col[:], 1.0))
        I("vector", lambda e: e.memset(ones_row_b[:], 1.0))
        I("vector", lambda e: e.memset(ones_row_f[:], 1.0))
        I("vector", lambda e: e.memset(sc1[0:1, 3:4], 0.0))
        I("vector", lambda e: e.memset(eps_t[:], EPS), DV)

        def batch8_transpose(srcs, dst3):
            W("tensor", DV, DV.n)
            for dc in range(8):
                I("tensor", lambda e, s2=srcs[dc], dc=dc: e.transpose(
                    PB0[:, 128 * dc:128 * (dc + 1)], s2, id_b[:]), PE if dc == 7 else None)
            W("vector", PE, PE.n)
            v = I("vector", lambda e, d=dst3: e.tensor_copy(
                d, PB0.ap().rearrange("p (d q) -> p d q", d=8)), DV)
            W("tensor", DV, v)
            return v

        stage_jobs = []
        stage_jobs += [("wdq%d" % i, w_dq[128 * i:128 * (i + 1), :], C, wdq_nat[i]) for i in range(2)]
        stage_jobs += [("wdkv%d" % i, w_dkv[128 * i:128 * (i + 1), :], C, wdkv_natb[i]) for i in range(2)]
        stage_jobs += [("wuq%d" % i, w_uq[128 * i:128 * (i + 1), :], QL, wuq_nat8[i]) for i in range(8)]
        stage_jobs += [("wuk%d" % i, w_uk[128 * i:128 * (i + 1), :], KVL, wuk_nat[i]) for i in range(8)]
        stage_jobs += [("wuv%d" % i, w_uv[128 * i:128 * (i + 1), :], KVL, wuv_nat[i]) for i in range(8)]
        stage_jobs += [("fa%d" % i, fa[128 * i:128 * (i + 1), :], 16, a_sb[:, 16 * i:16 * (i + 1)]) for i in range(8)]
        wo_b = [masks_sb[:, i * C:(i + 1) * C] for i in range(8)]
        wo_jobs = [("wo%d" % i, w_o[128 * i:128 * (i + 1), :], C, wo_b[i]) for i in range(8)]

        cast_of = {}
        stage_ctr = {}
        stage_prev = []

        def plan_stage(jobs, k=1, gate=None):
            key = id(jobs)
            stage_ctr.setdefault(key, 0)
            for _ in range(k):
                if stage_ctr[key] >= len(jobs):
                    return
                name, src2, ncol, dst = jobs[stage_ctr[key]]
                stage_ctr[key] += 1
                gi = len(stage_prev)
                buf = r2[gi % 2]
                if gi >= 2:
                    W("gpsimd", AC, cast_of[stage_prev[gi - 2]])
                if gate is not None:
                    W("gpsimd", PE, gate)
                sm = (SA, SB)[gi % 2]
                ld_v = I("gpsimd", lambda e, b=buf, s2=src2, n=ncol: e.dma_start(out=b[:, 0:n], in_=s2), sm, 16)
                W("scalar", sm, ld_v)
                cast_of[name] = I("scalar", lambda e, b=buf, d=dst, n=ncol: e.activation(
                    out=d, in_=b[:, 0:n], func=AF.Copy), AC)
                stage_prev.append(name)

        x_cast, x_trb, x_mm = {}, {}, {}

        def x_piece(t):
            if t >= 2:
                W("sync", AC, x_cast[t - 2])
            ld_v = I("sync", lambda e, t=t: e.dma_start(
                out=xq_f2[t % 2][:], in_=xk[128 * t:128 * (t + 1), :]), TR, 16)
            W("scalar", TR, ld_v)
            if t >= 2:
                W("scalar", PE, x_trb[t - 2])
            x_cast[t] = I("scalar", lambda e, t=t: e.activation(
                out=x1b[t % 2][:], in_=xq_f2[t % 2][:], func=AF.Copy), AC)
            W("tensor", AC, x_cast[t])
            W("tensor", DV, DV.n)
            for dc in range(8):
                I("tensor", lambda e, dc=dc, t=t: e.transpose(
                    PB0[:, 128 * dc:128 * (dc + 1)], x1b[t % 2][:, 128 * dc:128 * (dc + 1)],
                    id_b[:]), PE if dc == 7 else None)
            x_trb[t] = PE.n
            W("vector", PE, PE.n)
            if t >= 2:
                W("vector", PE, x_mm[t - 2])
            dv = I("vector", lambda e, t=t: e.tensor_copy(stair_ph0[t % 2], PB0[:]), DV)
            W("tensor", DV, dv)
            for latc in range(2):
                for dc in range(8):
                    I("tensor", lambda e, latc=latc, dc=dc, t=t: e.matmul(
                        PS_M[:, 128 * latc:128 * (latc + 1)],
                        wdkvT[:, 256 * dc + 128 * latc:256 * dc + 128 * (latc + 1)],
                        stair_ph0[t % 2][:, 128 * dc:128 * (dc + 1)],
                        start=(dc == 0), stop=(dc == 7), skip_group_check=True))
            for dc in range(8):
                I("tensor", lambda e, dc=dc, t=t: e.matmul(
                    PS_M[:, 256:512], stair_ph0[t % 2][:, 128 * dc:128 * (dc + 1)],
                    wdkvT[:, 256 * dc:256 * (dc + 1)],
                    start=(dc == 0), stop=(dc == 7), skip_group_check=True),
                    PE if dc == 7 else None)
            x_mm[t] = PE.n
            W("scalar", PE, PE.n)
            ac2 = None
            for latc in range(2):
                ac2 = I("scalar", lambda e, latc=latc, t=t: e.activation(
                    out=ckvT[latc][:, 128 * t:128 * (t + 1)],
                    in_=PS_M[:, 128 * latc:128 * (latc + 1)], func=AF.Copy), AC)
            W("vector", AC, ac2)
            dv = I("vector", lambda e, t=t: e.tensor_copy(
                ckvkm[:, 256 * t:256 * (t + 1)], PS_M[:, 256:512]), DV)
            W("tensor", AC, ac2)
            W("tensor", DV, dv)

        xq_f2 = xq_f

        qbufs = [out_st[0], out_st[1]]
        red_hist = []

        def plan_quant_load_reduce(i):
            buf = qbufs[i % 2]
            sm = (LQA, LQB)[i % 2]
            if i >= 2:
                W("sync", DV, red_hist[i - 2])
            ld_v = I("sync", lambda e, b=buf, i=i: e.dma_start(out=b[:, 0:C], in_=fk[128 * i:128 * (i + 1), :]), sm, 16)
            W("vector", sm, ld_v)
            v = I("vector", lambda e, b=buf, i=i: e.tensor_reduce(
                smax[:, i:i + 1], b[:, 0:C], AX.X, ALU.max, apply_absolute_value=True), DV)
            red_hist.append(v)

        def plan_quant_scale():
            W("vector", DV, red_hist[-1])
            VE(lambda e: e.tensor_reduce(smax[:, 8:9], smax[:, 0:8], AX.X, ALU.max))
            W("tensor", DV, DV.n)
            v = I("tensor", lambda e: e.transpose(PS_M[0:1, 0:128], smax[:, 8:9], id_f[:]), PE)
            W("vector", PE, v)
            VE(lambda e: e.tensor_reduce(sc1[0:1, 0:1], PS_M[0:1, 0:128], AX.X, ALU.max))
            VE(lambda e: e.tensor_scalar(out=sc1[0:1, 1:2], in0=sc1[0:1, 0:1],
                                         scalar1=1.0 / 7.0, scalar2=None, op0=ALU.mult))
            v = VE(lambda e: e.reciprocal(sc1[0:1, 2:3], sc1[0:1, 1:2]))
            W("tensor", DV, v)
            I("tensor", lambda e: e.matmul(PS_M[:, 0:1], ones_row_f[:], sc1[0:1, 1:2],
                                           start=True, stop=True, skip_group_check=True), PE)
            v = I("tensor", lambda e: e.matmul(PS_M[:, 1:2], ones_row_f[:], sc1[0:1, 2:3],
                                               start=True, stop=True, skip_group_check=True), PE)
            W("scalar", PE, v)
            I("scalar", lambda e: e.activation(out=m7[:], in_=PS_M[:, 0:1], func=AF.Copy), AC)
            v = I("scalar", lambda e: e.activation(out=r7[:], in_=PS_M[:, 1:2], func=AF.Copy), AC)
            W("vector", AC, v)
            W("tensor", AC, v)

        lad_hist = []

        def plan_quant_round(i):
            buf = qbufs[i % 2]
            sm = (LQA, LQB)[i % 2]
            if i >= 2:
                W("sync", DV, lad_hist[i - 2])
            ld_v = I("sync", lambda e, b=buf, i=i: e.dma_start(out=b[:, 0:C], in_=fk[128 * i:128 * (i + 1), :]), sm, 16)
            W("vector", sm, ld_v)
            VE(lambda e, b=buf: e.tensor_scalar(out=x1f_s[:, 0:C], in0=b[:, 0:C],
                                                scalar1=r7[:], scalar2=None, op0=ALU.mult))
            VE(lambda e: e.tensor_copy(qint[:, 0:C], x1f_s[:, 0:C]))
            v = VE(lambda e, i=i: e.tensor_scalar(
                out=wq[i][:], in0=qint[:, 0:C], scalar1=m7[:], scalar2=None, op0=ALU.mult))
            lad_hist.append(v)

        plan_stage(stage_jobs, 4)
        W("tensor", AC, cast_of["wdkv1"])
        for latc in range(2):
            batch8_transpose([wdkv_natb[latc][:, 128 * dc:128 * (dc + 1)] for dc in range(8)],
                             wdkvT.ap().rearrange("p (d t q) -> p d t q", t=2, q=128)[:, :, latc, :])
        for t in range(16):
            x_piece(t)
            plan_stage(stage_jobs, 2)
            if t == 2:
                W("tensor", AC, cast_of["wdq1"])
                for qlc in range(2):
                    batch8_transpose([wdq_nat[qlc][:, 128 * dc:128 * (dc + 1)] for dc in range(8)],
                                     wdqT.ap().rearrange("p (d t q) -> p d t q", t=2, q=128)[:, :, qlc, :])
            elif t == 5:
                W("tensor", AC, cast_of["wuq7"])
                for qlc in range(2):
                    batch8_transpose([wuq_nat8[dc][:, 128 * qlc:128 * (qlc + 1)] for dc in range(8)],
                                     wuqT[qlc].ap().rearrange("p (d q) -> p d q", d=8))
            elif t == 7:
                W("tensor", AC, cast_of["wuk7"])
                for qlc in range(2):
                    for dc in range(8):
                        I("tensor", lambda e, qlc=qlc, dc=dc: e.matmul(
                            PS_M[:, 0:256], wuq_nat8[dc][:, 128 * qlc:128 * (qlc + 1)], wuk_nat[dc],
                            start=(dc == 0), stop=(dc == 7), skip_group_check=True),
                            PE if dc == 7 else None)
                    W("scalar", PE, PE.n)
                    v = I("scalar", lambda e, qlc=qlc: e.activation(out=ke1[qlc], in_=PS_M[:, 0:256], func=AF.Copy), AC)
                    W("tensor", AC, v)
            elif t == 8:
                keff_pe = 0
                for cc in range(8):
                    for qlc in range(2):
                        I("tensor", lambda e, cc=cc, qlc=qlc: e.matmul(
                            PS_M[:, 0:256], wdq_nat[qlc][:, 128 * cc:128 * (cc + 1)], ke1[qlc],
                            start=(qlc == 0), stop=(qlc == 1), skip_group_check=True),
                            PE if qlc == 1 else None)
                    keff_pe = PE.n
                    W("scalar", PE, PE.n)
                    v = I("scalar", lambda e, cc=cc: e.activation(
                        out=keff[:, 256 * cc:256 * (cc + 1)], in_=PS_M[:, 0:256], func=AF.Copy), AC)
                    W("tensor", AC, v)
            elif t == 9:
                plan_stage(wo_jobs, 2, gate=keff_pe)
                plan_quant_load_reduce(0)
                plan_quant_load_reduce(1)
            elif t == 10:
                plan_stage(wo_jobs, 3)
                for i in range(2, 5):
                    plan_quant_load_reduce(i)
            elif t == 11:
                plan_stage(wo_jobs, 3)
                for i in range(5, 8):
                    plan_quant_load_reduce(i)
            elif t == 12:
                plan_quant_scale()
            elif t >= 13:
                plan_quant_round(2 * (t - 13))
                plan_quant_round(2 * (t - 13) + 1)
        plan_stage(stage_jobs, 99)
        for i in range(6, 8):
            plan_quant_round(i)
        W("gpsimd", AC, cast_of[stage_prev[-2]])
        ld_v = I("gpsimd", lambda e: e.dma_start(out=r2[0][0:16, 0:C], in_=fb), SA, 16)
        W("scalar", SA, ld_v)
        I("scalar", lambda e: e.activation(out=b_sb[:], in_=r2[0][0:16, 0:C], func=AF.Copy), AC)
        W("gpsimd", AC, cast_of[stage_prev[-1]])
        ld_v = I("gpsimd", lambda e: e.dma_start(out=r2[1][0:1, 0:C], in_=fbias), SB, 16)
        W("scalar", SB, ld_v)
        I("scalar", lambda e: e.activation(out=bias_sb[:], in_=r2[1][0:1, 0:C], func=AF.Copy), AC)
        W("tensor", AC, cast_of["wo7"])
        woT = []
        for dc in range(8):
            if dc < 4:
                woT.append([qTc(2 * dc), qTc(2 * dc + 1)])
            else:
                woT.append([xqTc(2 * (dc - 4)), xqTc(2 * (dc - 4) + 1)])
        for cc in range(8):
            half, off = divmod(128 * cc, QB)
            if cc == 0:
                W("tensor", DV, DV.n)
            for dc in range(8):
                I("tensor", lambda e, cc=cc, dc=dc: e.transpose(
                    PB0[:, 128 * dc:128 * (dc + 1)], wo_b[cc][:, 128 * dc:128 * (dc + 1)],
                    id_b[:]), PE if dc == 7 else None)
            W("scalar", PE, PE.n)
            v = None
            for dd in range(8):
                dst = (qT if dd < 4 else xqT)[:, 1024 * (dd % 4) + 512 * half + off:
                                              1024 * (dd % 4) + 512 * half + off + 128]
                v = I("scalar", lambda e, dd=dd, dst=dst: e.activation(
                    out=dst, in_=PB0[:, 128 * dd:128 * (dd + 1)], func=AF.Copy), AC)
            W("tensor", AC, v)
        W("tensor", AC, cast_of["wuv7"])
        for lc in range(2):
            for nb in range(2):
                for dc in range(8):
                    I("tensor", lambda e, lc=lc, nb=nb, dc=dc: e.matmul(
                        PS_M[:], wuv_nat[dc][:, 128 * lc:128 * (lc + 1)], woT[dc][nb],
                        start=(dc == 0), stop=(dc == 7), skip_group_check=True),
                        PE if dc == 7 else None)
                W("scalar", PE, PE.n)
                v = I("scalar", lambda e, lc=lc, nb=nb: e.activation(
                    out=vsb[lc][:, QB * nb:QB * (nb + 1)], in_=PS_M[:], func=AF.Copy), AC)
                W("tensor", AC, v)

        barrier()

        n_out_dma = 0
        last_stair_pe = [0]
        last_stair_dv = [0]
        qprep_pe_hold = [0]
        den_last = [0]
        q0_last = [0]
        for s in range(NSLOT):
            BS = 16 * s + 16
            W("gpsimd", DV, DV.n)
            W("gpsimd", AC, AC.n)
            mk_v = I("gpsimd", lambda e, s=s: e.dma_start(out=masks_sb[:], in_=masks[s]), LD, 16)

            if s == 0:
                W("sync", AC, AC.n)
                W("sync", DV, DV.n)
                pre_cast = []
                for tt in range(4):
                    buf = xq_f[tt % 2]
                    if tt >= 2:
                        W("sync", AC, pre_cast[tt - 2])
                    ld_v = I("sync", lambda e, b=buf, s=s, tt=tt: e.dma_start(
                        out=b[:], in_=xq[QB * s + 128 * tt: QB * s + 128 * (tt + 1), :]), TR, 16)
                    W("scalar", TR, ld_v)
                    W("scalar", PE, PE.n)
                    if tt == 0:
                        W("scalar", DV, DV.n)
                    v = I("scalar", lambda e, b=buf: e.activation(out=x1b[0][:], in_=b[:], func=AF.Copy), AC)
                    pre_cast.append(v)
                    W("tensor", AC, v)
                    W("tensor", DV, DV.n)
                    dst3 = xqT.ap().rearrange("p (d q) -> p d q", d=8)[:, :, 128 * tt:128 * (tt + 1)]
                    transpose_tile_to_xT(x1b[0], dst3)
                W("tensor", DV, DV.n)
                qlow = [y_sb[:, 0:QB], y_sb[:, QB:2 * QB]]
                for qlc in range(2):
                    for dc in range(8):
                        I("tensor", lambda e, qlc=qlc, dc=dc: e.matmul(
                            PS_M[:], wdqT[:, 256 * dc + 128 * qlc:256 * dc + 128 * (qlc + 1)], xqTc(dc),
                            start=(dc == 0), stop=(dc == 7), skip_group_check=True),
                            PE if dc == 7 else None)
                    W("scalar", PE, PE.n)
                    v = I("scalar", lambda e, qlc=qlc: e.activation(out=qlow[qlc], in_=PS_M[:], func=AF.Copy), AC)
                    W("tensor", AC, v)
                for dc in range(8):
                    for qlc in range(2):
                        I("tensor", lambda e, dc=dc, qlc=qlc: e.matmul(
                            PS_M[:], wuqT[qlc][:, 128 * dc:128 * (dc + 1)], qlow[qlc],
                            start=(qlc == 0), stop=(qlc == 1), skip_group_check=True),
                            PE if qlc == 1 else None)
                    W("scalar", PE, PE.n)
                    v = I("scalar", lambda e, dc=dc: e.activation(out=qTc(dc), in_=PS_M[:], func=AF.Copy), AC)
                    W("tensor", AC, v)
                qprep_pe_hold[0] = PE.n

            ac_ctxn, dv_den, ac_qlat = {}, {}, {}
            exp_last, pe_end = {}, {}
            c1_dv, c2_dv, b1_ac, b2_ac, b3_dv, b1_dv, c2_vr = {}, {}, {}, {}, {}, {}, {}
            psm_free = [[(AC, AC.n)]]

            def use_psm():
                for sm, v in psm_free[0]:
                    W("tensor", sm, v)

            def set_psm(sm, v):
                psm_free[0] = [(sm, v)]

            def set_psm_multi(pairs):
                psm_free[0] = list(pairs)

            def prologue_q(h, lc, q0sum=True):
                p = h % 2
                cc, po = h // 2, 64 * (h % 2)
                bank = PS_SUM if (lc == 0 and q0sum) else PS_M
                if lc == 0 and q0sum:
                    W("tensor", DV, den_last[0])
                else:
                    use_psm()
                v = I("tensor", lambda e, cc=cc, po=po, lc=lc, bank=bank: e.matmul(
                    bank[:], keff[po:po + 64, 256 * cc + 128 * lc:256 * cc + 128 * (lc + 1)],
                    qT[po:po + 64, QB * cc:QB * (cc + 1)],
                    start=True, stop=True, skip_group_check=True), PE)
                W("scalar", PE, v)
                v = I("scalar", lambda e, p=p, lc=lc, bank=bank: e.activation(
                    out=qlat[p][lc][:], in_=bank[:], func=AF.Copy), AC)
                if lc == 0 and q0sum:
                    q0_last[0] = v
                else:
                    set_psm(AC, v)
                    if lc == 1:
                        ac_qlat[h] = v

            def prologue_c1(h):
                p = h % 2
                sm = 3 * p
                W("vector", AC, ac_qlat[h])
                VE(lambda e, p=p, sm=sm: e.tensor_reduce(smax[:, sm:sm + 1], qlat[p][0][:], AX.X,
                                                         ALU.max, apply_absolute_value=True))
                VE(lambda e, p=p, sm=sm: e.tensor_reduce(smax[:, sm + 1:sm + 2], qlat[p][1][:], AX.X,
                                                         ALU.max, apply_absolute_value=True))
                c1_dv[h] = VE(lambda e, sm=sm: e.tensor_max(smax[:, sm + 2:sm + 3], smax[:, sm:sm + 1],
                                                            smax[:, sm + 1:sm + 2]))

            def prologue_c2(h):
                p = h % 2
                sm = 3 * p
                sx = 5 + 2 * p
                use_psm()
                W("tensor", DV, c1_dv[h])
                v = I("tensor", lambda e, sm=sm: e.transpose(PS_M[0:1, 0:128], smax[:, sm + 2:sm + 3],
                                                             id_f[:]), PE)
                W("vector", PE, v)
                vr = VE(lambda e, sx=sx: e.tensor_reduce(sc1[0:1, sx:sx + 1], PS_M[0:1, 0:128], AX.X, ALU.max))
                c2_vr[h] = vr
                set_psm(DV, vr)
                c2_dv[h] = VE(lambda e, sx=sx: e.tensor_mul(sc1[0:1, sx + 1:sx + 2], sc1[0:1, sx:sx + 1],
                                                            kfac[0:1, 0:1]))

            def prologue_c3(h):
                p = h % 2
                sx = 5 + 2 * p
                use_psm()
                W("tensor", DV, c2_dv[h])
                v = I("tensor", lambda e, sx=sx: e.matmul(PS_M[:, 0:1], ones_row_f[:],
                                                          sc1[0:1, sx + 1:sx + 2],
                                                          start=True, stop=True, skip_group_check=True), PE)
                W("scalar", PE, v)
                vv = SE(lambda e, h=h: e.activation(out=negc[:, h:h + 1], in_=PS_M[:, 0:1],
                                                    func=AF.Copy, scale=-1.0))
                set_psm(AC, vv)

            def epilogue_a(h, loop_pe):
                hp = h % 2
                W("scalar", PE, loop_pe)
                I("scalar", lambda e, hp=hp: e.activation(out=ctxn[hp][0][:], in_=PS_C[0][:], func=AF.Copy), AC)
                v = I("scalar", lambda e, hp=hp: e.activation(out=ctxn[hp][1][:], in_=PS_C[1][:], func=AF.Copy), AC)
                ac_ctxn[h] = v
                W("vector", PE, loop_pe)
                dv_den[h] = VE(lambda e, hp=hp: e.tensor_copy(den_sb[hp][:], PS_SUM[0:1, :]))
                den_last[0] = dv_den[h]

            def epilogue_b1(h):
                hp = h % 2
                b1_dv[h] = VE(lambda e, hp=hp: e.reciprocal(recip[hp][:], den_sb[hp][:]))

            def epilogue_b1b(h):
                hp = h % 2
                W("scalar", DV, b1_dv[h])
                b1_ac[h] = I("scalar", lambda e, hp=hp: e.activation(out=recip_b[hp][:], in_=recip[hp][:],
                                                                    func=AF.Copy), AC)

            def epilogue_b2(h):
                hp = h % 2
                use_psm()
                W("tensor", AC, b1_ac[h])
                v = I("tensor", lambda e, hp=hp: e.matmul(PS_M[:], ones_row_b[:], recip_b[hp][:],
                                                          start=True, stop=True, skip_group_check=True), PE)
                W("scalar", PE, v)
                b2_ac[h] = I("scalar", lambda e, hp=hp: e.activation(out=rbc[hp][:], in_=PS_M[:], func=AF.Copy), AC)
                set_psm(AC, b2_ac[h])

            def epilogue_b3(h):
                hp = h % 2
                W("vector", AC, b2_ac[h])
                I("vector", lambda e, hp=hp: e.tensor_mul(ctxn[hp][0][:], ctxn[hp][0][:], rbc[hp][:]), DV)
                b3_dv[h] = I("vector", lambda e, hp=hp: e.tensor_mul(ctxn[hp][1][:], ctxn[hp][1][:], rbc[hp][:]), DV)

            def epilogue_c(H):
                use_psm()
                W("tensor", DV, b3_dv[2 * H + 1])
                for hh in range(2):
                    hg = 2 * H + hh
                    for lc in range(2):
                        I("tensor", lambda e, hh=hh, hg=hg, lc=lc: e.matmul(
                            PS_M[64 * hh:64 * hh + 64, :],
                            vsb[lc][:, 64 * hg:64 * hg + 64], ctxn[hh][lc][:],
                            start=(lc == 0), stop=(lc == 1), skip_group_check=True),
                            PE if (hh == 1 and lc == 1) else None)
                W("scalar", PE, PE.n)
                v = I("scalar", lambda e, H=H: e.activation(
                    out=y_sb[:, QB * H:QB * (H + 1)], in_=PS_M[:], func=AF.Copy), AC)
                set_psm(AC, v)
                W("tensor", AC, v)

            slot_start_ac = AC.n
            slot_start_dv = DV.n
            stair_cast, stair_tr, stair_mm_pe, stair_dv = {}, {}, {}, {}

            def stair_dma(h):
                T = 16 * (s + 1) + h
                if h >= 2:
                    W("sync", AC, stair_cast[h - 2])
                else:
                    W("sync", AC, slot_start_ac)
                return I("sync", lambda e, T=T, h=h: e.dma_start(
                    out=xq_f[h % 2][:], in_=xk[128 * T:128 * (T + 1), :]), TR, 16)

            def stair_cast_f(h, ld_v):
                W("scalar", TR, ld_v)
                if h >= 2:
                    W("scalar", TR, stair_tr[h - 2])
                else:
                    W("scalar", DV, slot_start_dv)
                stair_cast[h] = I("scalar", lambda e, h=h: e.activation(
                    out=x1b[h % 2][:], in_=xq_f[h % 2][:], func=AF.Copy), AC)

            def stair_trans(h):
                W("sync", AC, stair_cast[h])
                if h >= 1:
                    W("sync", PE, stair_mm_pe[h - 1])
                else:
                    W("sync", PE, last_stair_pe[0])
                for dc in range(8):
                    I("sync", lambda e, dc=dc, h=h: e.dma_start(
                        out=stairT[:, 128 * dc:128 * (dc + 1)],
                        in_=x1b[h % 2][:, 128 * dc:128 * (dc + 1)], transpose=True), TR, 16)
                stair_tr[h] = TR.n

            def stair_mm(h):
                T = 16 * (s + 1) + h
                W("tensor", TR, stair_tr[h])
                W("tensor", DV, stair_dv.get(h - 1, last_stair_dv[0]))
                for dc in range(8):
                    I("tensor", lambda e, dc=dc: e.matmul(
                        PS_M[:, 256:512], stairT[:, 128 * dc:128 * (dc + 1)],
                        wdkvT[:, 256 * dc:256 * (dc + 1)],
                        start=(dc == 0), stop=(dc == 7), skip_group_check=True))
                W("tensor", DV, c2_vr.get(h + 1, c2_vr.get(h, 0)))
                for latc in range(2):
                    for dc in range(8):
                        I("tensor", lambda e, latc=latc, dc=dc: e.matmul(
                            PS_M[:, 128 * latc:128 * (latc + 1)],
                            wdkvT[:, 256 * dc + 128 * latc:256 * dc + 128 * (latc + 1)],
                            stairT[:, 128 * dc:128 * (dc + 1)],
                            start=(dc == 0), stop=(dc == 7), skip_group_check=True),
                            PE if (latc == 1 and dc == 7) else None)
                stair_mm_pe[h] = PE.n
                last_stair_pe[0] = PE.n
                W("scalar", PE, PE.n)
                ac2 = None
                for latc in range(2):
                    ac2 = I("scalar", lambda e, latc=latc, T=T: e.activation(
                        out=ckvT[latc][:, 128 * T:128 * (T + 1)],
                        in_=PS_M[:, 128 * latc:128 * (latc + 1)], func=AF.Copy), AC)
                W("vector", AC, ac2)
                dv = I("vector", lambda e, T=T: e.tensor_copy(
                    ckvkm[:, 256 * T:256 * (T + 1)], PS_M[:, 256:512]), DV)
                stair_dv[h] = dv
                last_stair_dv[0] = dv
                set_psm(DV, dv)

            c0, c1 = 2048 * s, 2048 * (s + 1)
            VE(lambda e, c0=c0, c1=c1: e.tensor_reduce(smax[:, 0:1], ckvT[0][:, c0:c1], AX.X,
                                                       ALU.max, apply_absolute_value=True))
            VE(lambda e, c0=c0, c1=c1: e.tensor_reduce(smax[:, 1:2], ckvT[1][:, c0:c1], AX.X,
                                                       ALU.max, apply_absolute_value=True))
            v = VE(lambda e: e.tensor_max(smax[:, 2:3], smax[:, 0:1], smax[:, 1:2]))
            W("tensor", DV, v)
            use_psm()
            v = I("tensor", lambda e: e.transpose(PS_M[0:1, 0:128], smax[:, 2:3], id_f[:]), PE)
            W("vector", PE, v)
            VE(lambda e: e.tensor_reduce(sc1[0:1, 9:10], PS_M[0:1, 0:128], AX.X, ALU.max))
            VE(lambda e: e.tensor_max(sc1[0:1, 3:4], sc1[0:1, 3:4], sc1[0:1, 9:10]))
            v = VE(lambda e: e.tensor_scalar(out=sc1[0:1, 4:5], in0=sc1[0:1, 3:4],
                                             scalar1=32.0, scalar2=None, op0=ALU.mult))
            W("tensor", DV, v)
            v = I("tensor", lambda e: e.matmul(PS_M[:, 0:1], ones_row_f[:], sc1[0:1, 4:5],
                                               start=True, stop=True, skip_group_check=True), PE)
            W("scalar", PE, v)
            kf_v = SE(lambda e: e.activation(out=kfac[:], in_=PS_M[:, 0:1], func=AF.Copy))
            set_psm(AC, kf_v)

            st_ld_next = [0]
            if s < 3:
                st_ld_next[0] = stair_dma(0)
            prologue_q(0, 0, q0sum=False)
            prologue_q(0, 1)
            prologue_c1(0)
            prologue_c2(0)
            prologue_c3(0)

            for h in range(NH):
                hp = h % 2
                p = h % 2
                exp_ac, msk_dv, s_pe, pacc_dv, sum_pe, ctx_pe = {}, {}, {}, {}, {}, {}

                def plan_S(kc, h=h, p=p, exp_ac=exp_ac, s_pe=s_pe):
                    par = kc % 4
                    if kc >= 4:
                        W("tensor", AC, exp_ac[kc - 4])
                    elif h > 0:
                        W("tensor", AC, exp_last[h - 1][kc % 4])
                    if kc == 0:
                        W("tensor", AC, ac_qlat.get(h, 0))
                    for lc in range(2):
                        I("tensor", lambda e, kc=kc, lc=lc, par=par, p=p: e.matmul(
                            PS_S[par], ckvT[lc][:, 128 * kc:128 * (kc + 1)], qlat[p][lc][:],
                            start=(lc == 0), stop=(lc == 1), skip_group_check=True),
                            PE if lc == 1 else None)
                    s_pe[kc] = PE.n

                def plan_exp(kc, h=h, s=s, exp_ac=exp_ac, msk_dv=msk_dv,
                             s_pe=s_pe, pacc_dv=pacc_dv, ctx_pe=ctx_pe):
                    par = kc % 4
                    if kc >= 4:
                        W("scalar", PE, ctx_pe[kc - 4])
                        W("scalar", DV, pacc_dv[kc - 4])
                    else:
                        W("scalar", PE, s_pe[kc])
                        if h > 0:
                            W("scalar", DV, dv_den[h - 1])
                    v = I("scalar", lambda e, par=par, h=h: e.activation(
                        out=PT[par][:], in_=PS_S[par], func=AF.Exp, scale=0.125,
                        bias=negc[:, h:h + 1]), AC)
                    exp_ac[kc] = v
                    W("vector", AC, v)
                    if kc >= 16 * s:
                        w_i = kc - 16 * s
                        msk_dv[kc] = I("vector", lambda e, par=par, w_i=w_i: e.tensor_mul(
                            PT[par][:], PT[par][:], masks_sb[:, 512 * w_i:512 * (w_i + 1)]), DV)
                    g = kc % 8
                    pb = (kc // 8) % 2
                    if g == 0:
                        gi = kc // 8
                        if gi >= 2:
                            W("vector", PE, sum_pe[gi - 2])
                        elif h > 0:
                            W("vector", PE, pe_end[h - 1])
                        pacc_dv[kc] = I("vector", lambda e, par=par, pb=pb: e.tensor_copy(
                            Pacc[pb][:], PT[par][:]), DV)
                    else:
                        pacc_dv[kc] = I("vector", lambda e, par=par, pb=pb: e.tensor_add(
                            Pacc[pb][:], Pacc[pb][:], PT[par][:]), DV)

                def plan_ctx(kc, h=h, BS=BS, exp_ac=exp_ac, msk_dv=msk_dv,
                             pacc_dv=pacc_dv, sum_pe=sum_pe, ctx_pe=ctx_pe):
                    par = kc % 4
                    if kc in msk_dv:
                        W("tensor", DV, msk_dv[kc])
                    else:
                        W("tensor", AC, exp_ac[kc])
                    first, last = (kc == 0), (kc == BS - 1)
                    if first and h > 0:
                        W("tensor", AC, ac_ctxn[h - 1])
                        W("tensor", DV, dv_den[h - 1])
                    for lc in range(2):
                        I("tensor", lambda e, kc=kc, lc=lc, par=par: e.matmul(
                            PS_C[lc][:], ckvkm[:, 256 * kc + 128 * lc:256 * kc + 128 * (lc + 1)],
                            PT[par][:], start=first, stop=last, skip_group_check=True),
                            PE if lc == 1 else None)
                    ctx_pe[kc] = PE.n
                    if kc % 8 == 7:
                        pb = (kc // 8) % 2
                        W("tensor", DV, pacc_dv[kc])
                        if kc == 7:
                            W("tensor", AC, q0_last[0])
                        I("tensor", lambda e, pb=pb, kc=kc, BS=BS: e.matmul(
                            PS_SUM[0:1, :], ones_col[:], Pacc[pb][:],
                            start=(kc == 7), stop=(kc == BS - 1), skip_group_check=True), PE)
                        sum_pe[kc // 8] = PE.n

                if h == 0:
                    W("scalar", LD, mk_v)
                for k0 in range(min(4, BS)):
                    plan_S(k0)
                    plan_exp(k0)
                for kc in range(BS):
                    if kc + 4 < BS:
                        plan_S(kc + 4)
                    plan_ctx(kc)
                    if kc + 4 < BS:
                        plan_exp(kc + 4)
                    if kc == 1 and h > 0:
                        epilogue_b1(h - 1)
                    elif kc == 2 and s < 3:
                        stair_cast_f(h, st_ld_next[0])
                    elif kc == 6 and s < 3 and h + 1 < NH:
                        st_ld_next[0] = stair_dma(h + 1)
                    elif kc == 3 and h + 1 < NH:
                        prologue_q(h + 1, 0, q0sum=False)
                    elif kc == 4:
                        if s < 3:
                            stair_trans(h)
                        if h > 0:
                            epilogue_b1b(h - 1)
                    elif kc == 5 and h + 1 < NH:
                        prologue_q(h + 1, 1)
                        prologue_c1(h + 1)
                    elif kc == 9 and h + 1 < NH:
                        prologue_c2(h + 1)
                    elif kc == 8 and h > 0:
                        epilogue_b2(h - 1)
                    elif kc == 10 and h > 0:
                        epilogue_b3(h - 1)
                    elif kc == 15 and h >= 2 and h % 2 == 0:
                        epilogue_c((h - 2) // 2)
                    elif kc == 13 and s < 3:
                        stair_mm(h)
                    elif kc == 14 and h + 1 < NH:
                        prologue_c3(h + 1)
                exp_last[h] = {kc % 4: exp_ac[kc] for kc in (BS - 4, BS - 3, BS - 2, BS - 1)}
                pe_end[h] = PE.n
                epilogue_a(h, PE.n)
            epilogue_b1(NH - 1)
            epilogue_b1b(NH - 1)
            epilogue_b2(NH - 1)
            epilogue_b3(NH - 1)
            epilogue_c(NH // 2 - 1)
            ysb_ac = AC.n

            def plan_ffn(tt):
                par = tt % 2
                if tt >= 1:
                    W("tensor", DV, r2_dv[tt - 1])
                else:
                    W("tensor", AC, ac_ctxn[NH - 1])
                    W("tensor", DV, dv_den[NH - 1])
                W("tensor", AC, uT_ac[tt])
                for fbk in range(2):
                    bank = PS_C[fbk]
                    for dc in range(8):
                        I("tensor", lambda e, dc=dc, fbk=fbk, bank=bank: e.matmul(
                            bank[:], x1T[:, 128 * dc:128 * (dc + 1)], wq[dc][:, QB * fbk:QB * (fbk + 1)],
                            start=(dc == 0), stop=False, skip_group_check=True))
                    I("tensor", lambda e, fbk=fbk, bank=bank: e.matmul(
                        bank[:], uT[:], b_sb[:, QB * fbk:QB * (fbk + 1)],
                        start=False, stop=False, skip_group_check=True))
                    I("tensor", lambda e, fbk=fbk, bank=bank: e.matmul(
                        bank[:], ones_row_b[:], bias_sb[0:1, QB * fbk:QB * (fbk + 1)],
                        start=False, stop=True, skip_group_check=True), PE)
                    W("vector", PE, PE.n)
                    I("vector", lambda e, fbk=fbk, par=par, bank=bank: e.tensor_add(
                        r2[par][:, QB * fbk:QB * (fbk + 1)], x1b[par][:, QB * fbk:QB * (fbk + 1)],
                        bank[:]), DV)
                return DV.n

            def plan_ln(src, scratch, b0):
                sq = I("scalar", lambda e, src=src, scratch=scratch: e.activation(
                    out=scratch, in_=src, func=AF.Square, accum_out=stats[:, b0:b0 + 1]), AC)
                VE(lambda e, src=src: e.tensor_reduce(stats[:, b0 + 1:b0 + 2], src, AX.X, ALU.add))
                W("vector", AC, sq)
                VE(lambda e: e.tensor_scalar(out=stats[:, b0 + 4:b0 + 5], in0=stats[:, b0 + 1:b0 + 2],
                                             scalar1=stats[:, b0 + 1:b0 + 2], scalar2=1.0 / (1024.0 * 1024.0),
                                             op0=ALU.mult, op1=ALU.mult))
                v = VE(lambda e: e.scalar_tensor_tensor(
                    out=stats[:, b0 + 5:b0 + 6], in0=stats[:, b0:b0 + 1], scalar=1.0 / 1024,
                    in1=stats[:, b0 + 4:b0 + 5], op0=ALU.mult, op1=ALU.subtract))
                W("scalar", DV, v)
                SE(lambda e: e.activation(out=stats[:, b0 + 6:b0 + 7], in_=stats[:, b0 + 5:b0 + 6],
                                          func=AF.Ln, bias=eps_t[:]))
                v = SE(lambda e: e.activation(out=stats[:, b0 + 7:b0 + 8], in_=stats[:, b0 + 6:b0 + 7],
                                              func=AF.Exp, scale=-0.5))
                W("vector", AC, v)
                v = VE(lambda e: e.scalar_tensor_tensor(
                    out=stats[:, b0 + 9:b0 + 10], in0=stats[:, b0 + 1:b0 + 2], scalar=-1.0 / 1024,
                    in1=stats[:, b0 + 7:b0 + 8], op0=ALU.mult, op1=ALU.mult))
                W("scalar", DV, v)

            ln1_ac, x1T_dv, r2_dv, uT_ac = {}, {}, {}, {}
            r1_dv = {}
            prep_cast2, prep_tr2, pld_of = {}, {}, {}
            pa_dv0 = DV.n
            for tt in range(5):
                par = tt % 2
                if tt >= 1:
                    pv = tt - 1
                    parv = pv % 2
                    W("tensor", AC, ln1_ac[pv])
                    if pv >= 1:
                        W("tensor", DV, r1_dv[pv])
                    for dc in range(8):
                        I("tensor", lambda e, dc=dc, parv=parv: e.transpose(
                            PB0[:, 128 * dc:128 * (dc + 1)], x1b[parv][:, 128 * dc:128 * (dc + 1)],
                            id_b[:]), PE if dc == 7 else None)
                    W("vector", PE, PE.n)
                    x1T_dv[pv] = I("vector", lambda e: e.tensor_copy(x1T[:], PB0[:]), DV)
                if tt < 4:
                    if tt >= 2:
                        W("sync", AC, ln1_ac[tt - 2])
                        W("sync", DV, x1T_dv.get(tt - 2, 0))
                    elif s < 3:
                        W("sync", AC, stair_cast[14 + tt])
                    ld_v = I("sync", lambda e, s=s, tt=tt, par=par: e.dma_start(
                        out=xq_f[par][:], in_=xq[QB * s + 128 * tt: QB * s + 128 * (tt + 1), :]), TR, 16)
                    if s + 1 < NSLOT:
                        def stg_of(k):
                            return masks_sb.ap().bitcast(F32)[:, 1024 * (k % 2):1024 * (k % 2 + 1)]

                        def pld_issue(k):
                            if k < 2:
                                W("sync", DV, pa_dv0)
                            else:
                                W("sync", AC, prep_cast2[k - 2])
                            pld_of[k] = I("sync", lambda e, stg=stg_of(k), s=s, k=k: e.dma_start(
                                out=stg, in_=xq[QB * (s + 1) + 128 * k: QB * (s + 1) + 128 * (k + 1), :]),
                                (TGA, TGB)[k % 2], 16)

                        if tt == 0:
                            pld_issue(0)
                            pld_issue(1)
                        W("scalar", (TGA, TGB)[tt % 2], pld_of[tt])
                        if tt == 0:
                            W("scalar", PE, last_stair_pe[0])
                        else:
                            W("scalar", TR, prep_tr2[tt - 1])
                        prep_cast2[tt] = I("scalar", lambda e, stg=stg_of(tt): e.activation(
                            out=stairT[:], in_=stg, func=AF.Copy), AC)
                        if tt + 2 < 4:
                            pld_issue(tt + 2)
                        W("sync", AC, prep_cast2[tt])
                        if tt == 0:
                            W("sync", PE, qprep_pe_hold[0])
                        for dc in range(8):
                            I("sync", lambda e, dc=dc, tt=tt: e.dma_start(
                                out=xqT[:, QB * dc + 128 * tt:QB * dc + 128 * (tt + 1)],
                                in_=stairT[:, 128 * dc:128 * (dc + 1)], transpose=True), TR, 16)
                        prep_tr2[tt] = TR.n
                    if tt == 0:
                        W("tensor", AC, ysb_ac)
                    else:
                        W("tensor", DV, x1T_dv[tt - 1])
                    for Hc in range(8):
                        I("tensor", lambda e, Hc=Hc, tt=tt: e.transpose(
                            PB0[:, 128 * Hc:128 * (Hc + 1)],
                            y_sb[:, QB * Hc + 128 * tt: QB * Hc + 128 * (tt + 1)], id_b[:]),
                            PE if Hc == 7 else None)
                    ytr_pe = PE.n
                if tt >= 1:
                    pv = tt - 1
                    use_psm()
                    W("tensor", DV, x1T_dv[pv])
                    for dc in range(8):
                        I("tensor", lambda e, dc=dc: e.matmul(
                            PS_M[0:16, 0:128], a_sb[:, 16 * dc:16 * (dc + 1)],
                            x1T[:, 128 * dc:128 * (dc + 1)],
                            start=(dc == 0), stop=(dc == 7), skip_group_check=True),
                            PE if dc == 7 else None)
                    W("scalar", PE, PE.n)
                    uT_ac[pv] = I("scalar", lambda e: e.activation(
                        out=uT[:], in_=PS_M[0:16, 0:128], func=AF.Copy), AC)
                    set_psm(AC, uT_ac[pv])
                if tt < 4:
                    W("vector", PE, ytr_pe)
                    W("vector", TR, ld_v)
                    r1_dv[tt] = I("vector", lambda e, par=par: e.tensor_add(
                        xq_f[par][:], xq_f[par][:], PB0[:]), DV)
                    W("scalar", DV, r1_dv[tt])
                    if tt >= 2:
                        W("scalar", DV, r2_dv[tt - 2])
                    elif s < 3:
                        W("scalar", TR, stair_tr[14 + tt])
                    plan_ln(xq_f[par][:], r2[par][:], 0)
                    v = I("scalar", lambda e, par=par: e.activation(
                        out=x1b[par][:], in_=xq_f[par][:], func=AF.Identity,
                        scale=stats[:, 7:8], bias=stats[:, 9:10]), AC)
                    ln1_ac[tt] = v
                if tt >= 1:
                    r2_dv[pv] = plan_ffn(pv)
                    W("scalar", DV, r2_dv[pv])
                    if n_out_dma >= 2:
                        W("scalar", OT, 16 * (n_out_dma - 1))
                    plan_ln(r2[parv][:], out_st[parv][:], 12)
                    ov = I("scalar", lambda e, parv=parv: e.activation(
                        out=out_st[parv][:], in_=r2[parv][:], func=AF.Identity,
                        scale=stats[:, 19:20], bias=stats[:, 21:22]), AC)
                    W("gpsimd", AC, ov)
                    I("gpsimd", lambda e, s=s, pv=pv, parv=parv: e.dma_start(
                        out=out_hbm[QB * s + 128 * pv: QB * s + 128 * (pv + 1), :],
                        in_=out_st[parv][:]), OT, 16)
                    n_out_dma += 1
            if s + 1 < NSLOT:
                W("tensor", TR, prep_tr2[3])
                qlow2 = [Pacc[0][:], Pacc[1][:]]
                for qlc in range(2):
                    use_psm()
                    for dc in range(8):
                        I("tensor", lambda e, qlc=qlc, dc=dc: e.matmul(
                            PS_M[:], wdqT[:, 256 * dc + 128 * qlc:256 * dc + 128 * (qlc + 1)], xqTc(dc),
                            start=(dc == 0), stop=(dc == 7), skip_group_check=True),
                            PE if dc == 7 else None)
                    W("scalar", PE, PE.n)
                    v = I("scalar", lambda e, qlc=qlc: e.activation(out=qlow2[qlc], in_=PS_M[:], func=AF.Copy), AC)
                    set_psm(AC, v)
                for dc in range(8):
                    use_psm()
                    for qlc in range(2):
                        I("tensor", lambda e, dc=dc, qlc=qlc: e.matmul(
                            PS_M[:], wuqT[qlc][:, 128 * dc:128 * (dc + 1)], qlow2[qlc],
                            start=(qlc == 0), stop=(qlc == 1), skip_group_check=True),
                            PE if qlc == 1 else None)
                    W("scalar", PE, PE.n)
                    v = I("scalar", lambda e, dc=dc: e.activation(out=qTc(dc), in_=PS_M[:], func=AF.Copy), AC)
                    set_psm(AC, v)
                qprep_pe_hold[0] = PE.n

        W("gpsimd", OT, OT.n)

        blk = E(nc.Block())

        @blk.tensor
        def _(eng):
            for fn in plan["tensor"]:
                fn(eng)

        @blk.scalar
        def _(eng):
            for fn in plan["scalar"]:
                fn(eng)

        @blk.vector
        def _(eng):
            for fn in plan["vector"]:
                fn(eng)

        @blk.sync
        def _(eng):
            for fn in plan["sync"]:
                fn(eng)

        @blk.gpsimd
        def _(eng):
            for fn in plan["gpsimd"]:
                fn(eng)

    return nc


def _host_masks(c):
    m = np.zeros((NSLOT, 128, 16 * 512), dtype=np.float32)
    tri = (np.arange(128)[:, None] <= np.arange(128)[None, :]).astype(np.float32)
    for s in range(NSLOT):
        for w in range(16):
            for j in range(4):
                mt = 4 * c + j
                if w < mt:
                    m[s, :, 512 * w + 128 * j: 512 * w + 128 * (j + 1)] = 1.0
                elif w == mt:
                    m[s, :, 512 * w + 128 * j: 512 * w + 128 * (j + 1)] = tri
    return m.astype(BF)


def _gather_rows(c):
    idx = []
    for s in range(NSLOT):
        for j in range(4):
            mt = 16 * s + 4 * c + j
            idx.append(np.arange(128 * mt, 128 * (mt + 1)))
    return np.concatenate(idx)


def kernel(L0_attn_x, L0_attn_W_dq, L0_attn_W_uq, L0_attn_W_dkv, L0_attn_W_uk,
           L0_attn_W_uv, L0_attn_W_o, L0_ffn_kernel, L0_ffn_bias, L0_ffn_a, L0_ffn_b):
    x = np.ascontiguousarray(np.asarray(L0_attn_x, dtype=np.float32))
    common = {
        "w_dq": np.ascontiguousarray(np.asarray(L0_attn_W_dq, np.float32)),
        "w_uq": np.ascontiguousarray(np.asarray(L0_attn_W_uq, np.float32)),
        "w_dkv": np.ascontiguousarray(np.asarray(L0_attn_W_dkv, np.float32)),
        "w_uk": np.ascontiguousarray(np.asarray(L0_attn_W_uk, np.float32)),
        "w_uv": np.ascontiguousarray(np.asarray(L0_attn_W_uv, np.float32)),
        "w_o": np.ascontiguousarray(np.asarray(L0_attn_W_o, np.float32)),
        "fk": np.ascontiguousarray(np.asarray(L0_ffn_kernel, np.float32)),
        "fbias": np.ascontiguousarray(np.asarray(L0_ffn_bias, np.float32).reshape(1, C)),
        "fa": np.ascontiguousarray(np.asarray(L0_ffn_a, np.float32)),
        "fb": np.ascontiguousarray(np.asarray(L0_ffn_b, np.float32)),
        "ident_b": np.eye(128, dtype=BF),
        "ident_f": np.eye(128, dtype=np.float32),
    }
    in_maps = []
    gathers = []
    for j in range(8):
        b, c = j // 4, j % 4
        rows = _gather_rows(c)
        gathers.append((b, rows))
        in_maps.append({
            **common,
            "xk": np.ascontiguousarray(x[b]),
            "xq": np.ascontiguousarray(x[b][rows]),
            "masks": _host_masks(c),
        })
    nc = build_nc()
    res = run_bass_kernel_spmd(nc, in_maps, core_ids=list(range(8)))
    out = np.empty((B, T, C), dtype=np.float32)
    for j in range(8):
        b, rows = gathers[j]
        out[b][rows] = np.asarray(res.results[j]["out"], dtype=np.float32)
    return out
